# version 02
# baseline (speedup 1.0000x reference)
from contextlib import ExitStack
import numpy as np
import concourse.bass as bass
import concourse.mybir as mybir
from concourse.alu_op_type import AluOpType
from concourse.bass_utils import run_bass_kernel_spmd

F32 = mybir.dt.float32
BF16 = mybir.dt.bfloat16
AF = mybir.ActivationFunctionType
AX = mybir.AxisListType

B, T, D, DS, DR = 2, 4096, 2048, 16, 64
DSL = D // 8
NCH, TS = 8, 512
CH = B * TS
EPS = 1e-5


def bc_target(c, sg):
    full = c * 36
    within = 0
    for s in range(sg + 1):
        within += 2 + (2 if s % 8 == 0 else 0)
    return 16 * (full + within)


def build_parta(debug=False):
    nc = bass.Bass("TRN2", target_bir_lowering=False, detect_race_conditions=False)
    x_h = nc.dram_tensor("x", [B, T, D], F32, kind="ExternalInput")
    xsl_h = nc.dram_tensor("x_sl", [B, T, DSL], F32, kind="ExternalInput")
    wx_h = nc.dram_tensor("wx", [D, 96], F32, kind="ExternalInput")
    wdt_h = nc.dram_tensor("wdt_sl", [DR, DSL], F32, kind="ExternalInput")
    bdt_h = nc.dram_tensor("bdt_pp", [128, 2], F32, kind="ExternalInput")
    alog_h = nc.dram_tensor("alog_pp", [128, 32], F32, kind="ExternalInput")
    dp_h = nc.dram_tensor("dp_pp", [128, 2], F32, kind="ExternalInput")
    osel_h = nc.dram_tensor("osel", [128, 16 * 128], BF16, kind="ExternalInput")
    id_h = nc.dram_tensor("ident", [128, 128], F32, kind="ExternalInput")
    y_h = nc.dram_tensor("y_r", [DSL, B, T], F32, kind="ExternalOutput")
    if debug:
        dlnx_h = nc.dram_tensor("dbg_lnx0", [128, CH], F32, kind="ExternalOutput")
        dxz_h = nc.dram_tensor("dbg_xzT0", [96, CH], F32, kind="ExternalOutput")
        ddt_h = nc.dram_tensor("dbg_dt0", [128, CH], F32, kind="ExternalOutput")
        da_h = nc.dram_tensor("dbg_a7", [128, T], F32, kind="ExternalOutput")
        dh_h = nc.dram_tensor("dbg_h6", [128, T], F32, kind="ExternalOutput")

    scr_dt = [nc.dram_tensor(f"scr_dt{c}", [B, 8, 32, TS], F32) for c in range(NCH)]
    scr_dtu = [nc.dram_tensor(f"scr_dtu{c}", [B, 8, 32, TS], F32) for c in range(NCH)]
    scr_bc = [nc.dram_tensor(f"scr_bc{c}", [2, DS, CH], F32) for c in range(NCH)]

    with ExitStack() as ctx:
        sb = lambda nm, shp, dt=F32: ctx.enter_context(nc.sbuf_tensor(nm, shp, dt))
        ps = lambda nm, shp: ctx.enter_context(nc.psum_tensor(nm, shp, F32))
        sem = lambda nm: ctx.enter_context(nc.semaphore(nm))

        xrow = [sb(f"xrow{i}", [128, D]) for i in range(2)]
        xslrow = [sb(f"xslrow{i}", [128, DSL]) for i in range(2)]
        xT = [sb(f"xT{k}", [128, CH]) for k in range(16)]
        uraw = [sb(f"uraw{k}", [128, CH]) for k in range(2)]
        bnbuf = sb("bnbuf", [128, 24])
        mvbuf = sb("mvbuf", [128, 2])
        vecol = sb("vecol", [128, 1])
        sdcol = sb("sdcol", [128, 1])
        rcol = sb("rcol", [128, 1])
        wx_sb = [sb(f"wx{k}", [128, 96]) for k in range(16)]
        wdt_sb = sb("wdt", [DR, DSL])
        bdt_sb = sb("bdt", [128, 2])
        negA_sb = sb("negA", [128, 32])
        negA2 = sb("negA2", [128, 32])
        dp_sb = sb("dp", [128, 2])
        osel_sb = ctx.enter_context(nc.sbuf_tensor("osel_sb", [128, 16 * 128], BF16))
        id_sb = sb("id_sb", [128, 128])
        xzT = sb("xzT", [96, CH])
        espb = sb("espb", [128, CH])
        dt_sb = [sb(f"dt_sb{k}", [128, CH]) for k in range(2)]
        dtu_sb = [sb(f"dtu_sb{k}", [128, CH]) for k in range(2)]
        bb_sb = [sb(f"bb_sb{b}", [128, TS]) for b in range(B)]
        cb_sb = [sb(f"cb_sb{b}", [128, TS]) for b in range(B)]
        GSZ = 4 * TS
        dtb = sb("dtb", [128, GSZ])
        dtub = sb("dtub", [128, GSZ])
        abuf = sb("abuf", [128, GSZ])
        xbuf = sb("xbuf", [128, GSZ])
        hbuf = [sb(f"hbuf{i}", [128, GSZ]) for i in range(2)]
        chbuf = ctx.enter_context(nc.sbuf_tensor("chbuf", [128, GSZ], BF16))
        hcar = sb("hcar", [128, 64])
        ysb = sb("ysb", [128, TS])

        ptr = [ps(f"ptr{i}", [128, 128]) for i in range(2)]
        pxz = [ps(f"pxz{i}", [96, TS]) for i in range(2)]
        pdt = [ps(f"pdt{i}", [128, TS]) for i in range(2)]
        py = [ps(f"py{i}", [128, TS]) for i in range(2)]

        s_ld = sem("s_ld")
        s_init = sem("s_init")
        s_x = [sem("s_x0"), sem("s_x1")]
        s_xsl = [sem("s_xsl0"), sem("s_xsl1")]
        s_bn = sem("s_bn")
        s_sd = sem("s_sd")
        s_ln = sem("s_ln")
        s_trq = sem("s_trq")
        s_evq = sem("s_evq")
        s_te = sem("s_te")
        s_ve = sem("s_ve")
        s_sc = sem("s_sc")
        s_scw = sem("s_scw")
        s_bc = sem("s_bc")
        s_bcu = sem("s_bcu")
        s_bcb = sem("s_bcb")
        s_bcc = sem("s_bcc")
        s_aex = sem("s_aex")
        s_dve = sem("s_dve")
        s_ymm = sem("s_ymm")
        s_yst = sem("s_yst")
        s_yw = sem("s_yw")
        s_dbg = sem("s_dbg")

        block = ctx.enter_context(nc.Block())

        @block.sync
        def _(sync):
            for h, t in [(wdt_h, wdt_sb), (bdt_h, bdt_sb), (alog_h, negA_sb),
                         (dp_h, dp_sb), (osel_h, osel_sb), (id_h, id_sb)]:
                sync.dma_start(out=t[:, :], in_=h.ap()).then_inc(s_ld, 16)
            for k in range(16):
                sync.dma_start(out=wx_sb[k][:, :],
                               in_=bass.AP(wx_h, k * 128 * 96, [[96, 128], [1, 96]])
                               ).then_inc(s_ld, 16)

        @block.scalar
        def _(scalar):
            scalar.wait_ge(s_ld, 16 * 22)
            scalar.activation(negA_sb[:, :], negA_sb[:, :], AF.Exp).then_inc(s_init, 1)

        @block.vector
        def _(vector):
            vector.wait_ge(s_init, 1)
            vector.memset(hcar[:, :], 0.0)
            vector.tensor_scalar(negA2[:, :], negA_sb[:, :], -1.0, None,
                                 AluOpType.mult).then_inc(s_init, 1)

        n_dbg = 0
        for c in range(NCH):
            for rt in range(8):
                gidx = 8 * c + rt
                s_i, r0 = rt // 4, TS * c + 128 * (rt % 4)
                col0 = s_i * TS + 128 * (rt % 4)

                @block.sync
                def _(sync, gidx=gidx, rt=rt, s_i=s_i, r0=r0):
                    if gidx >= 2:
                        sync.wait_ge(s_trq, 18 * (gidx - 1))
                    sync.dma_start(out=xrow[rt % 2][:, :],
                                   in_=bass.AP(x_h, (s_i * T + r0) * D,
                                               [[D, 128], [1, D]])).then_inc(s_x[gidx % 2], 16)
                    sync.dma_start(out=xslrow[rt % 2][:, :],
                                   in_=bass.AP(xsl_h, (s_i * T + r0) * DSL,
                                               [[DSL, 128], [1, DSL]])).then_inc(s_xsl[gidx % 2], 16)

                @block.vector
                def _(vector, gidx=gidx, rt=rt):
                    vector.wait_ge(s_x[gidx % 2], 16 * (gidx // 2 + 1))
                    for q in range(4):
                        vector.bn_stats(bnbuf[:, 6 * q:6 * (q + 1)],
                                        xrow[rt % 2][:, TS * q:TS * (q + 1)])
                    vector.drain()
                    vector.bn_aggr(mvbuf[:, :], bnbuf[:, :])
                    vector.drain()
                    vector.tensor_scalar(vecol[:, :], mvbuf[:, 1:2], EPS, None,
                                         AluOpType.add).then_inc(s_bn, 1)

                @block.scalar
                def _(scalar, gidx=gidx):
                    scalar.wait_ge(s_bn, gidx + 1)
                    scalar.sqrt(sdcol[:, :], vecol[:, :]).then_inc(s_sd, 1)

                @block.vector
                def _(vector, gidx=gidx, rt=rt):
                    vector.wait_ge(s_sd, gidx + 1)
                    vector.reciprocal(rcol[:, :], sdcol[:, :])
                    vector.drain()
                    vector.tensor_scalar(xrow[rt % 2][:, :], xrow[rt % 2][:, :],
                                         mvbuf[:, 0:1], rcol[:, :],
                                         AluOpType.subtract, AluOpType.mult)
                    vector.wait_ge(s_xsl[gidx % 2], 16 * (gidx // 2 + 1))
                    vector.tensor_scalar(xslrow[rt % 2][:, :], xslrow[rt % 2][:, :],
                                         mvbuf[:, 0:1], rcol[:, :],
                                         AluOpType.subtract, AluOpType.mult
                                         ).then_inc(s_ln, 1)

                @block.tensor
                def _(tensor, gidx=gidx, rt=rt):
                    tensor.wait_ge(s_ln, gidx + 1)
                    for j in range(18):
                        if gidx * 18 + j >= 2:
                            tensor.wait_ge(s_evq, gidx * 18 + j - 1)
                        src = (xrow[rt % 2][:, 128 * j:128 * (j + 1)] if j < 16
                               else xslrow[rt % 2][:, 128 * (j - 16):128 * (j - 15)])
                        tensor.transpose(ptr[j % 2][:, :], src,
                                         id_sb[:, :]).then_inc(s_trq, 1)

                @block.scalar
                def _(scalar, gidx=gidx, rt=rt, col0=col0, c=c):
                    for j in range(18):
                        scalar.wait_ge(s_trq, gidx * 18 + j + 1)
                        if j == 16 and c > 0 and rt == 0:
                            scalar.wait_ge(s_yst, 4 * c)
                        dst = xT[j] if j < 16 else uraw[j - 16]
                        scalar.copy(dst[:, col0:col0 + 128],
                                    ptr[j % 2][:, :]).then_inc(s_evq, 1)

            @block.tensor
            def _(tensor, c=c):
                tensor.wait_ge(s_evq, 144 * (c + 1))
                for half in range(2):
                    for k in range(16):
                        ins = tensor.matmul(pxz[half][:, :], wx_sb[k][:, 0:96],
                                            xT[k][:, TS * half:TS * (half + 1)],
                                            start=(k == 0), stop=(k == 15))
                        if k == 15:
                            ins.then_inc(s_te, 1)

            @block.vector
            def _(vector, c=c):
                vector.wait_ge(s_te, 4 * c + 2)
                if c > 0:
                    vector.wait_ge(s_scw, 144 * c)
                vector.tensor_copy(xzT[:, 0:TS], pxz[0][:, :])
                vector.tensor_copy(xzT[:, TS:CH], pxz[1][:, :]).then_inc(s_ve, 1)

            for dt_i in range(2):
                @block.tensor
                def _(tensor, c=c, dt_i=dt_i):
                    if dt_i == 0:
                        tensor.wait_ge(s_ve, 2 * c + 1)
                    else:
                        tensor.wait_ge(s_sc, 2 * c + 1)
                    for half in range(2):
                        ins = tensor.matmul(pdt[half][:, :],
                                            wdt_sb[0:64, 128 * dt_i:128 * (dt_i + 1)],
                                            xzT[0:64, TS * half:TS * (half + 1)],
                                            start=True, stop=True)
                    ins.then_inc(s_te, 1)

                @block.scalar
                def _(scalar, c=c, dt_i=dt_i):
                    scalar.wait_ge(s_te, 4 * c + 3 + dt_i)
                    if c > 0 and dt_i == 0:
                        scalar.wait_ge(s_scw, 144 * c)
                    for half in range(2):
                        sl = slice(TS * half, TS * (half + 1))
                        scalar.activation(espb[:, sl], pdt[half][:, :],
                                          AF.Exp, bias=bdt_sb[:, dt_i:dt_i + 1])
                        scalar.drain()
                        ins = scalar.activation(dt_sb[dt_i][:, sl], espb[:, sl],
                                                AF.Ln, bias=1.0)
                    ins.then_inc(s_sc, 1)

            @block.vector
            def _(vector, c=c):
                vector.wait_ge(s_sc, 2 * c + 2)
                if c > 0:
                    vector.wait_ge(s_scw, 144 * c)
                for dt_i in range(2):
                    ins = vector.tensor_tensor(dtu_sb[dt_i][:, :], dt_sb[dt_i][:, :],
                                               uraw[dt_i][:, :], AluOpType.mult)
                ins.then_inc(s_ve, 1)

            @block.gpsimd
            def _(g, c=c):
                g.wait_ge(s_ve, 2 * c + 2)
                for dt_i in range(2):
                    for bb in range(B):
                        doff = bb * (8 * 32 * TS) + dt_i * 16 * TS
                        dap = [[TS, 16], [32 * TS, 8], [1, TS]]
                        g.dma_start(out=bass.AP(scr_dt[c], doff, dap),
                                    in_=dt_sb[dt_i][:, bb * TS:(bb + 1) * TS]
                                    ).then_inc(s_scw, 16)
                        g.dma_start(out=bass.AP(scr_dtu[c], doff, dap),
                                    in_=dtu_sb[dt_i][:, bb * TS:(bb + 1) * TS]
                                    ).then_inc(s_scw, 16)
                g.dma_start(out=bass.AP(scr_bc[c], 0, [[CH, 32], [1, CH]]),
                            in_=xzT[64:96, :]).then_inc(s_scw, 16)

            for sg in range(16):
                b, de = sg // 8, sg % 8
                sgi = 16 * c + sg
                ygrp = 4 * c + sg // 4

                @block.gpsimd
                def _(g, b=b, de=de, sgi=sgi, c=c):
                    g.wait_ge(s_scw, 144 * (c + 1))
                    if sgi >= 1:
                        g.wait_ge(s_dve, sgi)
                    soff = b * (8 * 32 * TS) + de * 4 * TS
                    sap = [[32 * TS, 8], [0, 16], [1, 4 * TS]]
                    g.dma_start(out=dtb[:, :],
                                in_=bass.AP(scr_dt[c], soff, sap)).then_inc(s_bc, 16)
                    g.dma_start(out=dtub[:, :],
                                in_=bass.AP(scr_dtu[c], soff, sap)).then_inc(s_bcu, 16)
                    if de == 0:
                        g.dma_start(out=bb_sb[b][:, :],
                                    in_=bass.AP(scr_bc[c], b * TS,
                                                [[0, 8], [CH, 16], [1, TS]])
                                    ).then_inc(s_bcb, 16)
                        g.dma_start(out=cb_sb[b][:, :],
                                    in_=bass.AP(scr_bc[c], 16 * CH + b * TS,
                                                [[0, 8], [CH, 16], [1, TS]])
                                    ).then_inc(s_bcc, 16)

                @block.scalar
                def _(scalar, b=b, de=de, sgi=sgi, c=c, sg=sg):
                    scalar.wait_ge(s_bc, 16 * (sgi + 1))
                    if sgi == 0:
                        scalar.wait_ge(s_init, 2)
                    for gp in range(4):
                        G = de * 4 + gp
                        ins = scalar.activation(abuf[:, TS * gp:TS * (gp + 1)],
                                                dtb[:, TS * gp:TS * (gp + 1)], AF.Exp,
                                                scale=negA2[:, G:G + 1])
                        if gp == 3:
                            ins.then_inc(s_aex, 1)

                @block.vector
                def _(vector, b=b, de=de, sgi=sgi, c=c, sg=sg):
                    vector.wait_ge(s_aex, sgi + 1)
                    vector.wait_ge(s_bcu, 16 * (sgi + 1))
                    vector.wait_ge(s_bcb, 16 * (2 * c + b + 1))
                    vector.wait_ge(s_bcc, 16 * (2 * c + b + 1))
                    if sgi >= 1:
                        vector.wait_ge(s_ymm, sgi)
                    for gp in range(4):
                        sl = slice(TS * gp, TS * (gp + 1))
                        vector.tensor_tensor(xbuf[:, sl], dtub[:, sl],
                                             bb_sb[b][:, :], AluOpType.mult)
                    vector.drain()
                    for gp in range(4):
                        G = de * 4 + gp
                        sl = slice(TS * gp, TS * (gp + 1))
                        vector.tensor_tensor_scan(
                            hbuf[sg % 2][:, sl], abuf[:, sl], xbuf[:, sl],
                            hcar[:, b * 32 + G:b * 32 + G + 1],
                            AluOpType.mult, AluOpType.add)
                    vector.drain()
                    for gp in range(4):
                        G = de * 4 + gp
                        vector.tensor_copy(hcar[:, b * 32 + G:b * 32 + G + 1],
                                           hbuf[sg % 2][:, TS * (gp + 1) - 1:TS * (gp + 1)])
                    vector.drain()
                    for gp in range(4):
                        sl = slice(TS * gp, TS * (gp + 1))
                        ins = vector.tensor_tensor(chbuf[:, sl], hbuf[sg % 2][:, sl],
                                                   cb_sb[b][:, :], AluOpType.mult)
                        if gp == 3:
                            ins.then_inc(s_dve, 1)

                @block.tensor
                def _(tensor, b=b, de=de, sgi=sgi, ygrp=ygrp, sg=sg):
                    tensor.wait_ge(s_dve, sgi + 1)
                    if de % 4 == 0 and ygrp >= 2:
                        tensor.wait_ge(s_yst, ygrp - 1)
                    for gp in range(4):
                        gl = (de % 4) * 4 + gp
                        ins = tensor.matmul(py[(sg // 4) % 2][:, :],
                                            osel_sb[:, 128 * gl:128 * (gl + 1)],
                                            chbuf[:, TS * gp:TS * (gp + 1)],
                                            start=(gl == 0), stop=(gl == 15))
                        if gp == 3:
                            ins.then_inc(s_ymm, 1)

                if de % 4 == 3:
                    dh = de // 4

                    @block.vector
                    def _(vector, b=b, dh=dh, ygrp=ygrp, sgi=sgi):
                        vector.wait_ge(s_ymm, sgi + 1)
                        if ygrp >= 1:
                            vector.wait_ge(s_yw, 16 * ygrp)
                        vector.scalar_tensor_tensor(
                            ysb[:, :], uraw[dh][:, b * TS:(b + 1) * TS],
                            dp_sb[:, dh:dh + 1], py[ygrp % 2][:, :],
                            AluOpType.mult, AluOpType.add).then_inc(s_yst, 1)

                    @block.sync
                    def _(sync, b=b, dh=dh, ygrp=ygrp, c=c):
                        sync.wait_ge(s_yst, ygrp + 1)
                        sync.dma_start(
                            out=bass.AP(y_h, (dh * 128) * B * T + b * T + TS * c,
                                        [[B * T, 128], [1, TS]]),
                            in_=ysb[:, :]).then_inc(s_yw, 16)

            if debug:
                @block.sync
                def _(sync, c=c):
                    sync.wait_ge(s_dve, 16 * c + 16)
                    if c == 0:
                        sync.dma_start(out=dlnx_h.ap(), in_=xT[0][:, :]).then_inc(s_dbg, 16)
                        sync.dma_start(out=dxz_h.ap(), in_=xzT[:, :]).then_inc(s_dbg, 16)
                        sync.dma_start(out=ddt_h.ap(), in_=dt_sb[0][:, :]).then_inc(s_dbg, 16)
                    sync.dma_start(out=bass.AP(da_h, TS * c, [[T, 128], [1, TS]]),
                                   in_=abuf[:, 0:TS]).then_inc(s_dbg, 16)
                    sync.dma_start(out=bass.AP(dh_h, TS * c, [[T, 128], [1, TS]]),
                                   in_=hbuf[0][:, 0:TS]).then_inc(s_dbg, 16)
                n_dbg += 5 if c == 0 else 2

        @block.sync
        def _(sync):
            sync.wait_ge(s_yw, 16 * 4 * NCH)
            if debug:
                sync.wait_ge(s_dbg, 16 * n_dbg)

    return nc



def osel_const():
    o = np.zeros((128, 16, 128), np.float32)
    for p in range(128):
        for gl in range(16):
            o[p, gl, gl * 8 + (p >> 4)] = 1.0
    import ml_dtypes
    return np.ascontiguousarray(o.reshape(128, 16 * 128)).astype(ml_dtypes.bfloat16)


def pp_layout(v_sl):
    return np.ascontiguousarray(v_sl.reshape(2, 128).T)


def alog_layout(a_sl):
    a = a_sl.reshape(32, 8, 16)
    a = a.transpose(0, 1, 2).reshape(32, 128)
    return np.ascontiguousarray(a.T)


def make_in_maps_a(d):
    x = np.ascontiguousarray(d["L0_attn_x"], dtype=np.float32)
    Wx = np.ascontiguousarray(d["L0_attn_Wx"], dtype=np.float32)
    Wdt = np.ascontiguousarray(d["L0_attn_Wdt"], dtype=np.float32)
    bdt = np.asarray(d["L0_attn_bdt"], dtype=np.float32)
    alog = np.asarray(d["L0_attn_A_log"], dtype=np.float32)
    Dp = np.asarray(d["L0_attn_Dp"], dtype=np.float32)
    osel = osel_const()
    ident = np.eye(128, dtype=np.float32)
    ims = []
    for r in range(8):
        sl = slice(DSL * r, DSL * (r + 1))
        ims.append({
            "x": x,
            "x_sl": np.ascontiguousarray(x[:, :, sl]),
            "wx": Wx,
            "wdt_sl": np.ascontiguousarray(Wdt[:, sl]),
            "bdt_pp": pp_layout(bdt[sl]),
            "alog_pp": alog_layout(alog[sl]),
            "dp_pp": pp_layout(Dp[sl]),
            "osel": osel,
            "ident": ident,
        })
    return ims





D, QKV, FFN, E = 2048, 3072, 4096, 8
NE = 8
STAGE = 99
RPC = 512
NQ, QC = 2, 512
HC = QC // 2
EPS = 1e-5
SC128 = float(1.0 / np.sqrt(128.0))


class Sem:
    def __init__(self, ctx, nc, name):
        self.h = ctx.enter_context(nc.semaphore(name))
        self.n = 0

    def inc(self, ins, k=1):
        ins.then_inc(self.h, k)
        self.n += k
        return ins


def build_partb(debug=False):
    nc = bass.Bass("TRN2", target_bir_lowering=False, detect_race_conditions=False)
    xr_h = nc.dram_tensor("x_rows", [2, RPC, D], F32, kind="ExternalInput")
    y_h = nc.dram_tensor("y_sl", [D, 2, RPC], F32, kind="ExternalInput")
    qkvT_h = nc.dram_tensor("qkvT", [D, QKV], F32, kind="ExternalInput")
    projT_h = nc.dram_tensor("projT", [D, D], F32, kind="ExternalInput")
    rw_h = nc.dram_tensor("router_w", [D, E], F32, kind="ExternalInput")
    w1_h = nc.dram_tensor("moe_w1", [E, D, FFN], F32, kind="ExternalInput")
    w2_h = nc.dram_tensor("moe_w2", [E, FFN, D], F32, kind="ExternalInput")
    ln1w_h = nc.dram_tensor("ln1w_pp", [128, 16], F32, kind="ExternalInput")
    ln1b_h = nc.dram_tensor("ln1b_pp", [128, 16], F32, kind="ExternalInput")
    ln2w_h = nc.dram_tensor("ln2w_pp", [128, 16], F32, kind="ExternalInput")
    ln2b_h = nc.dram_tensor("ln2b_pp", [128, 16], F32, kind="ExternalInput")
    id_h = nc.dram_tensor("ident", [128, 128], F32, kind="ExternalInput")
    hsel_h = nc.dram_tensor("hsel", [128, 256], F32, kind="ExternalInput")
    out_h = nc.dram_tensor("out_r", [2, RPC, D], F32, kind="ExternalOutput")
    attw_d = nc.dram_tensor("attw_scr", [2, 16, HC], F32)
    gat_d = nc.dram_tensor("gat_scr", [E, QC], F32)

    with ExitStack() as ctx:
        sb = lambda nm, shp, dt=F32: ctx.enter_context(nc.sbuf_tensor(nm, shp, dt))
        S = lambda nm: Sem(ctx, nc, nm)

        xrow = [sb(f"xrow{i}", [128, D]) for i in range(2)]
        ybuf = [sb(f"ybuf{i}", [128, 16 * 128], BF16) for i in range(2)]
        x1b = sb("x1b", [128, 16 * QC], BF16)
        lnx1b = sb("lnx1b", [128, 16 * QC], BF16)
        mmin = sb("mmin", [128, 16 * QC], BF16)
        qb = sb("qb", [128, 24 * QC], BF16)
        yg = sb("yg", [128, 32 * QC], BF16)
        tmph = sb("tmph", [128, 16 * QC])
        gelb = [sb("gelb0", [128, QC], BF16)] * 2
        prodb = [sb(f"prodb{i}", [128, HC], BF16) for i in range(2)]
        wq = [sb(f"wq{i}", [128, 16 * 128], BF16) for i in range(2)]
        w2s = [sb("w2s0", [128, 32 * 128], BF16)] * 2
        wbb = sb("wbb", [128, 16 * HC], BF16)
        gbsb = [sb(f"gbsb{i}", [128, QC], BF16) for i in range(2)]
        ln1w = sb("ln1w", [128, 16]); ln1bb = sb("ln1bb", [128, 16])
        ln2w = sb("ln2w", [128, 16]); ln2bb = sb("ln2bb", [128, 16])
        rw_sb = sb("rw_sb", [128, 16 * E], BF16)
        id_sb = sb("id_sb", [128, 128])
        hsel_sb = sb("hsel_sb", [128, 256], BF16)
        onesb = sb("onesb", [128, 1], BF16)
        onesrow = sb("onesrow", [1, 128])
        att = sb("att", [16, 6 * HC])
        rows = sb("rows", [1, 3 * QC])
        attw = sb("attw", [16, 2 * HC])
        gtok = sb("gtok", [128, 5 * E])
        gcol = sb("gcol", [128, 6])
        neg30 = sb("neg30", [128, E])
        gater = sb("gater", [E, QC])

        psn = lambda nm: ctx.enter_context(nc.psum_tensor(nm, [128, QC], F32))
        pA = psn("pA"); pB = psn("pB"); pC = psn("pC"); pD = psn("pD")
        pE = psn("pE"); pF = psn("pF"); pG = psn("pG"); pH = psn("pH")

        s_ld = S("s_ld")
        s_ldg = S("s_ldg")
        s_xr = [S("s_xr0"), S("s_xr1")]
        s_yl = [S("s_yl0"), S("s_yl1")]
        s_wq = [S("s_wq0"), S("s_wq1")]
        s_w2 = [S("s_w20"), S("s_w21")]
        s_gb = [S("s_gb0"), S("s_gb1")]
        s_aw = S("s_aw")
        s_awg = S("s_awg")
        s_te = S("s_te")
        s_sc = S("s_sc")
        s_ve = S("s_ve")
        s_ow = S("s_ow")

        block = ctx.enter_context(nc.Block())

        @block.sync
        def _(sync):
            for h, t in [(ln1w_h, ln1w), (ln1b_h, ln1bb), (ln2w_h, ln2w),
                         (ln2b_h, ln2bb), (id_h, id_sb)]:
                s_ld.inc(sync.dma_start(out=t[:, :], in_=h.ap()), 16)

        @block.gpsimd
        def _(g):
            s_ldg.inc(g.dma_start(out=hsel_sb[:, :], in_=hsel_h.ap()), 16)
            s_ldg.inc(g.dma_start(
                out=rw_sb[:, :],
                in_=bass.AP(rw_h, 0, [[E, 128], [128 * E, 16], [1, E]])), 16)

        @block.vector
        def _(vector):
            vector.wait_ge(s_ld.h, s_ld.n)
            vector.wait_ge(s_ldg.h, s_ldg.n)
            vector.memset(onesb[:, :], 1.0)
            vector.memset(onesrow[:, :], 1.0)
            s_ve.inc(vector.memset(neg30[:, :], -1e30))

        def stats_pair(src_bf, sq_dst):
            @block.scalar
            def _(scalar):
                scalar.wait_ge(s_ve.h, s_ve.n)
                for k in range(16):
                    ins = scalar.activation(sq_dst[:, k * QC:(k + 1) * QC],
                                            src_bf[:, k * QC:(k + 1) * QC], AF.Square)
                s_sc.inc(ins)

            @block.tensor
            def _(tensor):
                tensor.wait_ge(s_sc.h, s_sc.n)
                tensor.wait_ge(s_ve.h, s_ve.n)
                for k in range(16):
                    tensor.matmul(pE[0:1, :], onesb[:, :],
                                  src_bf[:, k * QC:(k + 1) * QC],
                                  start=(k == 0), stop=(k == 15))
                for k in range(16):
                    ins = tensor.matmul(pF[0:1, :], onesb[:, :],
                                        sq_dst[:, k * QC:(k + 1) * QC],
                                        start=(k == 0), stop=(k == 15))
                s_te.inc(ins)

            @block.vector
            def _(vector):
                vector.wait_ge(s_te.h, s_te.n)
                vector.tensor_scalar(rows[:, 0:QC], pE[0:1, :], 1.0 / D, None,
                                     AluOpType.mult)
                vector.tensor_scalar(rows[:, QC:2 * QC], pF[0:1, :], 1.0 / D, None,
                                     AluOpType.mult)
                vector.drain()
                vector.tensor_tensor(att[0:1, 0:QC], rows[:, 0:QC],
                                     rows[:, 0:QC], AluOpType.mult)
                vector.drain()
                vector.tensor_tensor(rows[:, QC:2 * QC], rows[:, QC:2 * QC],
                                     att[0:1, 0:QC], AluOpType.subtract)
                vector.drain()
                s_ve.inc(vector.tensor_scalar(rows[:, QC:2 * QC],
                                              rows[:, QC:2 * QC], EPS, None,
                                              AluOpType.add))

            @block.scalar
            def _(scalar):
                scalar.wait_ge(s_ve.h, s_ve.n)
                s_sc.inc(scalar.sqrt(att[0:1, 0:QC], rows[:, QC:2 * QC]))

            @block.vector
            def _(vector):
                vector.wait_ge(s_sc.h, s_sc.n)
                s_ve.inc(vector.reciprocal(rows[:, 2 * QC:3 * QC],
                                           att[0:1, 0:QC]))

            @block.tensor
            def _(tensor):
                tensor.wait_ge(s_ve.h, s_ve.n)
                tensor.matmul(pD[:, :], onesrow[0:1, :], rows[0:1, 0:QC],
                              start=True, stop=True)
                s_te.inc(tensor.matmul(pH[:, :], onesrow[0:1, :],
                                       rows[0:1, 2 * QC:3 * QC],
                                       start=True, stop=True))

        def mm_stream(w_dram, wtot, n_out, mov, ktiles, evac):
            te_after = {}
            ev_after = {}
            mov_ready = s_ve.n
            for o in range(n_out):
                @block.gpsimd
                def _(g, o=o):
                    if o >= 2:
                        g.wait_ge(s_te.h, te_after[o - 2])
                    s_wq[o % 2].inc(g.dma_start(
                        out=wq[o % 2][:, 0:ktiles * 128],
                        in_=bass.AP(w_dram, 128 * o,
                                    [[wtot, 128], [128 * wtot, ktiles], [1, 128]])), 16)

                @block.tensor
                def _(tensor, o=o, mov_ready=mov_ready):
                    tensor.wait_ge(s_wq[o % 2].h, s_wq[o % 2].n)
                    if o == 0:
                        tensor.wait_ge(s_ve.h, mov_ready)
                    if o >= 2:
                        tensor.wait_ge(*ev_after[o - 2])
                    pdst = pB if o % 2 == 0 else pC
                    for k in range(ktiles):
                        ins = tensor.matmul(pdst[:, :],
                                            wq[o % 2][:, 128 * k:128 * (k + 1)],
                                            mov[:, k * QC:(k + 1) * QC],
                                            start=(k == 0), stop=(k == ktiles - 1))
                    s_te.inc(ins)
                te_after[o] = s_te.n
                ev_after[o] = evac(o, pB if o % 2 == 0 else pC)

        def evac_qkv(o, psum):
            @block.scalar
            def _(scalar, o=o, psum=psum):
                scalar.wait_ge(s_te.h, s_te.n)
                s_sc.inc(scalar.activation(qb[:, o * QC:(o + 1) * QC], psum[:, :],
                                           AF.Copy))
            return (s_sc.h, s_sc.n)

        def evac_proj(o, psum):
            @block.vector
            def _(vector, o=o, psum=psum):
                vector.wait_ge(s_te.h, s_te.n)
                s_ve.inc(vector.tensor_tensor(tmph[:, o * QC:(o + 1) * QC],
                                              lnx1b[:, o * QC:(o + 1) * QC],
                                              psum[:, :], AluOpType.add))
            return (s_ve.h, s_ve.n)

        tr_cons = {}
        tr_sub = {}
        for q in range(NQ):

            for rt in range(4):
                gidx = 4 * q + rt
                s_i, r0 = rt // 2, HC * q + 128 * (rt % 2)
                colb = s_i * HC + 128 * (rt % 2)

                @block.sync
                def _(sync, gidx=gidx, rt=rt, s_i=s_i, r0=r0, q=q):
                    if gidx >= 2:
                        sync.wait_ge(s_ve.h, tr_cons[gidx - 2])
                    if q > 0 and rt < 2:
                        sync.wait_ge(s_ow.h, 16 * 4 * q)
                    s_xr[gidx % 2].inc(sync.dma_start(
                        out=xrow[rt % 2][:, :],
                        in_=bass.AP(xr_h, (s_i * RPC + r0) * D, [[D, 128], [1, D]])), 16)

                @block.gpsimd
                def _(g, gidx=gidx, rt=rt, s_i=s_i, r0=r0, q=q):
                    if gidx >= 2:
                        g.wait_ge(s_ve.h, tr_cons[gidx - 2])
                    s_yl[gidx % 2].inc(g.dma_start(
                        out=ybuf[rt % 2][:, :],
                        in_=bass.AP(y_h, s_i * RPC + r0,
                                    [[2 * RPC, 128], [128 * 2 * RPC, 16], [1, 128]])), 16)

                for sub in range(4):
                    @block.tensor
                    def _(tensor, gidx=gidx, rt=rt, sub=sub):
                        if sub == 0:
                            tensor.wait_ge(s_xr[gidx % 2].h, s_xr[gidx % 2].n)
                        if gidx * 4 + sub >= 1:
                            tensor.wait_ge(s_ve.h, tr_sub[gidx * 4 + sub - 1])
                        for jj in range(4):
                            j = 4 * sub + jj
                            ins = tensor.transpose(
                                pA[:, 128 * jj:128 * jj + 128],
                                xrow[rt % 2][:, 128 * j:128 * (j + 1)], id_sb[:, :])
                        s_te.inc(ins)

                    @block.vector
                    def _(vector, gidx=gidx, rt=rt, colb=colb, q=q, sub=sub):
                        vector.wait_ge(s_te.h, s_te.n)
                        if sub == 0:
                            vector.wait_ge(s_yl[gidx % 2].h, s_yl[gidx % 2].n)
                        for jj in range(4):
                            j = 4 * sub + jj
                            ins = vector.tensor_tensor(
                                x1b[:, j * QC + colb:j * QC + colb + 128],
                                pA[:, 128 * jj:128 * jj + 128],
                                ybuf[rt % 2][:, j * 128:(j + 1) * 128],
                                AluOpType.add)
                        s_ve.inc(ins)
                    tr_sub[gidx * 4 + sub] = s_ve.n
                tr_cons[gidx] = s_ve.n

            stats_pair(x1b, mmin)

            @block.vector
            def _(vector):
                vector.wait_ge(s_te.h, s_te.n)
                for k in range(16):
                    sl = slice(k * QC, (k + 1) * QC)
                    vector.tensor_tensor(lnx1b[:, sl], x1b[:, sl], pD[:, :],
                                         AluOpType.subtract)
                vector.drain()
                for k in range(16):
                    sl = slice(k * QC, (k + 1) * QC)
                    ins = vector.tensor_tensor(lnx1b[:, sl], lnx1b[:, sl], pH[:, :],
                                               AluOpType.mult)
                s_ve.inc(ins)

            stats_pair(lnx1b, mmin)

            @block.vector
            def _(vector):
                vector.wait_ge(s_te.h, s_te.n)
                for k in range(16):
                    sl = slice(k * QC, (k + 1) * QC)
                    vector.tensor_tensor(mmin[:, sl], lnx1b[:, sl], pD[:, :],
                                         AluOpType.subtract)
                vector.drain()
                for k in range(16):
                    sl = slice(k * QC, (k + 1) * QC)
                    vector.tensor_tensor(mmin[:, sl], mmin[:, sl], pH[:, :],
                                         AluOpType.mult)
                vector.drain()
                for k in range(16):
                    sl = slice(k * QC, (k + 1) * QC)
                    ins = vector.tensor_scalar(mmin[:, sl], mmin[:, sl],
                                               ln1w[:, k:k + 1], ln1bb[:, k:k + 1],
                                               AluOpType.mult, AluOpType.add)
                s_ve.inc(ins)

            if STAGE < 5: break
            mm_stream(qkvT_h, QKV, 24, mmin, 16, evac_qkv)

            if STAGE < 6: break
            for h in range(16):
                g = h // 4
                tq, tk, tv = 6 * g + h % 4, 6 * g + 4, 6 * g + 5

                @block.vector
                def _(vector, h=h, tq=tq, tk=tk):
                    vector.wait_ge(s_sc.h, s_sc.n)
                    if h >= 1:
                        vector.wait_ge(s_te.h, s_te.n)
                    vector.tensor_tensor(prodb[0][:, :], qb[:, tq * QC + HC:(tq + 1) * QC],
                                         qb[:, tk * QC:tk * QC + HC], AluOpType.mult)
                    s_ve.inc(vector.tensor_tensor(
                        prodb[1][:, :], qb[:, tq * QC + HC:(tq + 1) * QC],
                        qb[:, tk * QC + HC:(tk + 1) * QC], AluOpType.mult))

                @block.tensor
                def _(tensor, h=h):
                    tensor.wait_ge(s_ve.h, s_ve.n)
                    tensor.matmul(pE[0:16, 0:HC], hsel_sb[:, 16 * h:16 * (h + 1)],
                                  prodb[0][:, :], start=(h == 0), stop=(h == 15))
                    ins = tensor.matmul(pF[0:16, 0:HC], hsel_sb[:, 16 * h:16 * (h + 1)],
                                        prodb[1][:, :], start=(h == 0), stop=(h == 15))
                    s_te.inc(ins)

            @block.vector
            def _(vector):
                vector.wait_ge(s_te.h, s_te.n)
                vector.tensor_copy(att[:, 0:HC], pE[0:16, 0:HC])
                vector.drain()
                vector.tensor_tensor(att[:, 5 * HC:6 * HC], att[:, 0:HC],
                                     pF[0:16, 0:HC], AluOpType.max)
                vector.drain()
                vector.tensor_tensor(att[:, HC:2 * HC], att[:, 0:HC],
                                     att[:, 5 * HC:6 * HC], AluOpType.subtract)
                s_ve.inc(vector.tensor_tensor(att[:, 2 * HC:3 * HC], pF[0:16, 0:HC],
                                              att[:, 5 * HC:6 * HC],
                                              AluOpType.subtract))

            @block.scalar
            def _(scalar):
                scalar.wait_ge(s_ve.h, s_ve.n)
                scalar.activation(att[:, 3 * HC:4 * HC], att[:, HC:2 * HC], AF.Exp)
                s_sc.inc(scalar.activation(att[:, 4 * HC:5 * HC], att[:, 2 * HC:3 * HC],
                                           AF.Exp))

            @block.vector
            def _(vector):
                vector.wait_ge(s_sc.h, s_sc.n)
                vector.tensor_tensor(att[:, 5 * HC:6 * HC], att[:, 3 * HC:4 * HC],
                                     att[:, 4 * HC:5 * HC], AluOpType.add)
                vector.drain()
                vector.reciprocal(att[:, 0:HC], att[:, 5 * HC:6 * HC])
                vector.drain()
                vector.tensor_tensor(attw[:, 0:HC], att[:, 3 * HC:4 * HC],
                                     att[:, 0:HC], AluOpType.mult)
                s_ve.inc(vector.tensor_tensor(attw[:, HC:2 * HC], att[:, 4 * HC:5 * HC],
                                              att[:, 0:HC], AluOpType.mult))

            @block.sync
            def _(sync):
                sync.wait_ge(s_ve.h, s_ve.n)
                s_aw.inc(sync.dma_start(
                    out=bass.AP(attw_d, 0, [[HC, 16], [1, HC]]),
                    in_=attw[:, 0:HC]), 16)
                s_aw.inc(sync.dma_start(
                    out=bass.AP(attw_d, 16 * HC, [[HC, 16], [1, HC]]),
                    in_=attw[:, HC:2 * HC]), 16)

            @block.gpsimd
            def _(g, q=q):
                g.wait_ge(s_aw.h, s_aw.n)
                if q > 0:
                    g.wait_ge(s_ve.h, s_ve.n)
                s_awg.inc(g.dma_start(
                    out=wbb[:, :],
                    in_=bass.AP(attw_d, 0, [[0, 128], [HC, 16], [1, HC]])), 16)

            @block.vector
            def _(vector, q=q):
                vector.wait_ge(s_awg.h, s_awg.n)
                for h in range(16):
                    g2 = h // 4
                    tv = 6 * g2 + 5
                    ins = vector.tensor_tensor(
                        yg[:, h * QC:h * QC + HC], wbb[:, h * HC:(h + 1) * HC],
                        qb[:, tv * QC:tv * QC + HC], AluOpType.mult)
                s_ve.inc(ins)

            @block.gpsimd
            def _(g):
                g.wait_ge(s_ve.h, s_ve.n)
                s_awg.inc(g.dma_start(
                    out=wbb[:, :],
                    in_=bass.AP(attw_d, 16 * HC, [[0, 128], [HC, 16], [1, HC]])), 16)

            @block.vector
            def _(vector):
                vector.wait_ge(s_awg.h, s_awg.n)
                for h in range(16):
                    g2 = h // 4
                    tv = 6 * g2 + 5
                    vector.tensor_tensor(gelb[0][:, 0:HC], wbb[:, h * HC:(h + 1) * HC],
                                         qb[:, tv * QC + HC:(tv + 1) * QC],
                                         AluOpType.mult)
                    vector.drain()
                    vector.tensor_tensor(yg[:, h * QC + HC:(h + 1) * QC],
                                         yg[:, h * QC:h * QC + HC],
                                         gelb[0][:, 0:HC], AluOpType.add)
                    vector.drain()
                    ins = vector.tensor_copy(yg[:, h * QC:h * QC + HC],
                                             qb[:, tv * QC:tv * QC + HC])
                s_ve.inc(ins)

            if STAGE < 7: break
            mm_stream(projT_h, D, 16, yg, 16, evac_proj)

            @block.vector
            def _(vector):
                vector.wait_ge(s_ve.h, s_ve.n)
                for k in range(16):
                    ins = vector.tensor_copy(mmin[:, k * QC:(k + 1) * QC],
                                             tmph[:, k * QC:(k + 1) * QC])
                s_ve.inc(ins)
            stats_pair(mmin, qb)

            @block.vector
            def _(vector):
                vector.wait_ge(s_te.h, s_te.n)
                for k in range(16):
                    sl = slice(k * QC, (k + 1) * QC)
                    vector.tensor_tensor(tmph[:, sl], tmph[:, sl], x1b[:, sl],
                                         AluOpType.add)
                vector.drain()
                for k in range(16):
                    sl = slice(k * QC, (k + 1) * QC)
                    vector.tensor_tensor(mmin[:, sl], mmin[:, sl], pD[:, :],
                                         AluOpType.subtract)
                vector.drain()
                for k in range(16):
                    sl = slice(k * QC, (k + 1) * QC)
                    vector.tensor_tensor(mmin[:, sl], mmin[:, sl], pH[:, :],
                                         AluOpType.mult)
                vector.drain()
                for k in range(16):
                    sl = slice(k * QC, (k + 1) * QC)
                    ins = vector.tensor_scalar(mmin[:, sl], mmin[:, sl],
                                               ln2w[:, k:k + 1], ln2bb[:, k:k + 1],
                                               AluOpType.mult, AluOpType.add)
                s_ve.inc(ins)

            if STAGE < 9: break
            for tg in range(4):
                @block.tensor
                def _(tensor, tg=tg):
                    tensor.wait_ge(s_ve.h, s_ve.n)
                    for k in range(16):
                        ins = tensor.matmul(pG[0:128, 8 * tg:8 * (tg + 1)],
                                            mmin[:, k * QC + 128 * tg:k * QC + 128 * (tg + 1)],
                                            rw_sb[:, 8 * k:8 * (k + 1)],
                                            start=(k == 0), stop=(k == 15))
                    s_te.inc(ins)

                @block.vector
                def _(vector, tg=tg):
                    vector.wait_ge(s_te.h, s_te.n)
                    pgb = pG[0:128, 8 * tg:8 * (tg + 1)]
                    vector.tensor_copy(gtok[:, 0:E], pgb)
                    vector.drain()
                    vector.tensor_reduce(gcol[:, 0:1], gtok[:, 0:E], AX.X,
                                         AluOpType.max)
                    vector.drain()
                    vector.tensor_scalar(gtok[:, E:2 * E], gtok[:, 0:E],
                                         gcol[:, 0:1], None, AluOpType.is_ge)
                    vector.drain()
                    vector.scalar_tensor_tensor(gtok[:, 2 * E:3 * E],
                                                gtok[:, E:2 * E], -1e30,
                                                gtok[:, 0:E],
                                                AluOpType.mult, AluOpType.add)
                    vector.drain()
                    vector.tensor_reduce(gcol[:, 1:2], gtok[:, 2 * E:3 * E], AX.X,
                                         AluOpType.max)
                    s_ve.inc(vector.tensor_scalar(gcol[:, 4:5], gcol[:, 0:1], -1.0,
                                                  None, AluOpType.mult))

                @block.scalar
                def _(scalar, tg=tg):
                    scalar.wait_ge(s_ve.h, s_ve.n)
                    s_sc.inc(scalar.activation(gtok[:, 2 * E:3 * E], gtok[:, 0:E],
                                               AF.Exp, bias=gcol[:, 4:5]))

                @block.vector
                def _(vector, tg=tg):
                    vector.wait_ge(s_sc.h, s_sc.n)
                    vector.tensor_scalar(gtok[:, E:2 * E], gtok[:, 0:E],
                                         gcol[:, 1:2], None, AluOpType.is_ge)
                    vector.drain()
                    vector.tensor_tensor(gtok[:, 3 * E:4 * E], gtok[:, 2 * E:3 * E],
                                         gtok[:, E:2 * E], AluOpType.mult)
                    vector.drain()
                    vector.tensor_reduce(gcol[:, 2:3], gtok[:, 3 * E:4 * E], AX.X,
                                         AluOpType.add)
                    vector.drain()
                    vector.reciprocal(gcol[:, 3:4], gcol[:, 2:3])
                    vector.drain()
                    s_ve.inc(vector.tensor_scalar(gtok[:, 4 * E:5 * E],
                                                  gtok[:, 3 * E:4 * E],
                                                  gcol[:, 3:4], None, AluOpType.mult))

                @block.tensor
                def _(tensor, tg=tg):
                    tensor.wait_ge(s_ve.h, s_ve.n)
                    s_te.inc(tensor.transpose(pH[0:8, 128 * tg:128 * (tg + 1)],
                                              gtok[:, 4 * E:5 * E], id_sb[:, :]))

                @block.vector
                def _(vector, tg=tg):
                    vector.wait_ge(s_te.h, s_te.n)
                    s_ve.inc(vector.tensor_copy(gater[:, 128 * tg:128 * (tg + 1)],
                                                pH[0:8, 128 * tg:128 * (tg + 1)]))

            @block.sync
            def _(sync):
                sync.wait_ge(s_ve.h, s_ve.n)
                s_aw.inc(sync.dma_start(out=gat_d.ap(), in_=gater[:, :]), 16)

            if STAGE < 10: break
            te_w1 = {}
            sc_gel = {}
            ve_y1g = {}
            xf_ready = s_ve.n
            for e in range(NE):
                @block.gpsimd
                def _(g, e=e, q=q):
                    g.wait_ge(s_aw.h, s_aw.n)
                    if e >= 2:
                        g.wait_ge(s_ve.h, ve_y1g[e - 2])
                    elif q > 0 and e < 2:
                        g.wait_ge(s_ve.h, s_ve.n)
                    s_gb[e % 2].inc(g.dma_start(
                        out=gbsb[e % 2][:, :],
                        in_=bass.AP(gat_d, e * QC, [[0, 128], [1, QC]])), 16)

                for o2 in range(32):
                    @block.gpsimd
                    def _(g, e=e, o2=o2):
                        if (e, o2) != (0, 0) and (e * 32 + o2) >= 2:
                            g.wait_ge(s_te.h, te_w1[e * 32 + o2 - 2])
                        s_wq[o2 % 2].inc(g.dma_start(
                            out=wq[o2 % 2][:, :],
                            in_=bass.AP(w1_h, e * D * FFN + 128 * o2,
                                        [[FFN, 128], [128 * FFN, 16], [1, 128]])), 16)

                    @block.tensor
                    def _(tensor, e=e, o2=o2, xf_ready=xf_ready):
                        tensor.wait_ge(s_wq[o2 % 2].h, s_wq[o2 % 2].n)
                        gi = e * 32 + o2
                        if gi == 0:
                            tensor.wait_ge(s_ve.h, xf_ready)
                        if gi >= 2:
                            tensor.wait_ge(s_sc.h, sc_gel[gi - 2])
                        pdst = pB if o2 % 2 == 0 else pC
                        for k in range(16):
                            ins = tensor.matmul(pdst[:, :],
                                                wq[o2 % 2][:, 128 * k:128 * (k + 1)],
                                                mmin[:, k * QC:(k + 1) * QC],
                                                start=(k == 0), stop=(k == 15))
                        s_te.inc(ins)
                    te_w1[e * 32 + o2] = s_te.n

                    @block.scalar
                    def _(scalar, e=e, o2=o2):
                        scalar.wait_ge(s_te.h, s_te.n)
                        gi = e * 32 + o2
                        if gi >= 1:
                            scalar.wait_ge(s_ve.h, ve_y1g.get(("o2", gi - 1), 0))
                        pdst = pB if o2 % 2 == 0 else pC
                        s_sc.inc(scalar.activation(gelb[o2 % 2][:, :], pdst[:, :],
                                                   AF.Gelu))
                    sc_gel[e * 32 + o2] = s_sc.n

                    @block.vector
                    def _(vector, e=e, o2=o2):
                        vector.wait_ge(s_sc.h, s_sc.n)
                        vector.wait_ge(s_gb[e % 2].h, s_gb[e % 2].n)
                        s_ve.inc(vector.tensor_tensor(
                            yg[:, o2 * QC:(o2 + 1) * QC], gelb[o2 % 2][:, :],
                            gbsb[e % 2][:, :], AluOpType.mult))
                    ve_y1g[("o2", e * 32 + o2)] = s_ve.n

                ve_y1g[e] = s_ve.n

                te_w2 = {}
                ve_acc = {}
                for o in range(16):
                    @block.gpsimd
                    def _(g, e=e, o=o):
                        if o >= 1:
                            g.wait_ge(s_te.h, te_w2[o - 1])
                        elif e >= 1:
                            g.wait_ge(s_te.h, s_te.n)
                        s_w2[o % 2].inc(g.dma_start(
                            out=w2s[o % 2][:, :],
                            in_=bass.AP(w2_h, e * FFN * D + 128 * o,
                                        [[D, 128], [128 * D, 32], [1, 128]])), 16)

                    @block.tensor
                    def _(tensor, e=e, o=o):
                        tensor.wait_ge(s_w2[o % 2].h, s_w2[o % 2].n)
                        tensor.wait_ge(s_ve.h, ve_y1g[e])
                        if o >= 2:
                            tensor.wait_ge(s_ve.h, ve_acc[o - 2])
                        pdst = pB if o % 2 == 0 else pC
                        for k in range(32):
                            ins = tensor.matmul(pdst[:, :],
                                                w2s[o % 2][:, 128 * k:128 * (k + 1)],
                                                yg[:, k * QC:(k + 1) * QC],
                                                start=(k == 0), stop=(k == 31))
                        s_te.inc(ins)
                    te_w2[o] = s_te.n

                    @block.vector
                    def _(vector, e=e, o=o):
                        vector.wait_ge(s_te.h, s_te.n)
                        s_ve.inc(vector.tensor_tensor(
                            tmph[:, o * QC:(o + 1) * QC], tmph[:, o * QC:(o + 1) * QC],
                            (pB if o % 2 == 0 else pC)[:, :], AluOpType.add))
                    ve_acc[o] = s_ve.n

            if STAGE < 11: break
            osub = {}
            for tg2 in range(4):
                for sub in range(4):
                    @block.tensor
                    def _(tensor, tg2=tg2, q=q, sub=sub):
                        if sub == 0:
                            tensor.wait_ge(s_ve.h, s_ve.n)
                            if tg2 >= 2:
                                tensor.wait_ge(s_ow.h, 16 * (4 * q + tg2 - 1))
                        if True:
                            tensor.wait_ge(s_sc.h, osub.get((tg2, sub - 1), s_sc.n if sub == 0 else 0))
                        for kk in range(4):
                            k = 4 * sub + kk
                            ins = tensor.transpose(
                                pA[:, 128 * kk:128 * kk + 128],
                                tmph[:, k * QC + 128 * tg2:k * QC + 128 * (tg2 + 1)],
                                id_sb[:, :])
                        s_te.inc(ins)

                    @block.scalar
                    def _(scalar, tg2=tg2, sub=sub):
                        scalar.wait_ge(s_te.h, s_te.n)
                        for kk in range(4):
                            k = 4 * sub + kk
                            ins = scalar.copy(xrow[tg2 % 2][:, 128 * k:128 * (k + 1)],
                                              pA[:, 128 * kk:128 * kk + 128])
                        s_sc.inc(ins)
                    osub[(tg2, sub)] = s_sc.n

                @block.sync
                def _(sync, tg2=tg2, q=q):
                    sync.wait_ge(s_sc.h, s_sc.n)
                    s_i = tg2 // 2
                    r0 = HC * q + 128 * (tg2 % 2)
                    s_ow.inc(sync.dma_start(
                        out=bass.AP(out_h, (s_i * RPC + r0) * D, [[D, 128], [1, D]]),
                        in_=xrow[tg2 % 2][:, :]), 16)

        @block.sync
        def _(sync):
            sync.wait_ge(s_ow.h, s_ow.n)

    return nc



def pp16(v):
    return np.ascontiguousarray(np.asarray(v, np.float32).reshape(16, 128).T)


def hsel_const():
    h = np.zeros((128, 16, 16), np.float32)
    for hh in range(16):
        h[:, hh, hh] = SC128
    return np.ascontiguousarray(h.reshape(128, 256))


def make_in_maps_b(d, y_full):
    qkvT = np.ascontiguousarray(np.asarray(d["L0_ffn_qkv_weight"], np.float32).T)
    projT = np.ascontiguousarray(np.asarray(d["L0_ffn_proj_weight"], np.float32).T)
    ims = []
    x = np.asarray(d["L0_attn_x"], np.float32)
    for r in range(8):
        rs = slice(RPC * r, RPC * (r + 1))
        ims.append({
            "x_rows": np.ascontiguousarray(x[:, rs, :]),
            "y_sl": np.ascontiguousarray(y_full[:, :, rs]),
            "qkvT": qkvT,
            "projT": projT,
            "router_w": np.ascontiguousarray(d["L0_ffn_router_weight"], dtype=np.float32),
            "moe_w1": np.ascontiguousarray(d["L0_ffn_moe_w1"], dtype=np.float32),
            "moe_w2": np.ascontiguousarray(d["L0_ffn_moe_w2"], dtype=np.float32),
            "ln1w_pp": pp16(d["L0_ffn_ln1_weight"]),
            "ln1b_pp": pp16(d["L0_ffn_ln1_bias"]),
            "ln2w_pp": pp16(d["L0_ffn_ln2_weight"]),
            "ln2b_pp": pp16(d["L0_ffn_ln2_bias"]),
            "ident": np.eye(128, dtype=np.float32),
            "hsel": hsel_const(),
        })
    return ims




def kernel(**inputs):
    d = inputs
    ims_a = make_in_maps_a(d)
    nc_a = build_parta(debug=False)
    res_a = run_bass_kernel_spmd(nc_a, ims_a, core_ids=list(range(8)))
    y_full = np.concatenate([np.asarray(res_a.results[r]["y_r"]).reshape(DSL, 2, T)
                             for r in range(8)], axis=0)
    ims_b = make_in_maps_b(d, y_full)
    nc_b = build_partb(debug=False)
    res_b = run_bass_kernel_spmd(nc_b, ims_b, core_ids=list(range(8)))
    out = np.concatenate([np.asarray(res_b.results[r]["out_r"]).reshape(2, RPC, D)
                          for r in range(8)], axis=1)
    return np.ascontiguousarray(out, dtype=np.float32)


# version 04
# speedup vs baseline: 1.1077x; 1.1077x over previous
from contextlib import ExitStack
import numpy as np
import concourse.bass as bass
import concourse.mybir as mybir
from concourse.alu_op_type import AluOpType
from concourse.bass_utils import run_bass_kernel_spmd

F32 = mybir.dt.float32
BF16 = mybir.dt.bfloat16
AF = mybir.ActivationFunctionType
AX = mybir.AxisListType

B, T, D, DS, DR = 2, 4096, 2048, 16, 64
DSL = D // 8
NCH, TS = 8, 512
CH = B * TS
EPS = 1e-5


def bc_target(c, sg):
    full = c * 36
    within = 0
    for s in range(sg + 1):
        within += 2 + (2 if s % 8 == 0 else 0)
    return 16 * (full + within)


def emit_parta(nc, semctx, debug=False):
    x_h = nc.dram_tensor("x", [B, T, D], F32, kind="ExternalInput")
    xsl_h = nc.dram_tensor("x_sl", [B, T, DSL], F32, kind="ExternalInput")
    wx_h = nc.dram_tensor("wx", [D, 96], F32, kind="ExternalInput")
    wdt_h = nc.dram_tensor("wdt_sl", [DR, DSL], F32, kind="ExternalInput")
    bdt_h = nc.dram_tensor("bdt_pp", [128, 2], F32, kind="ExternalInput")
    alog_h = nc.dram_tensor("alog_pp", [128, 32], F32, kind="ExternalInput")
    dp_h = nc.dram_tensor("dp_pp", [128, 2], F32, kind="ExternalInput")
    osel_h = nc.dram_tensor("osel", [128, 16 * 128], BF16, kind="ExternalInput")
    id_h = nc.dram_tensor("ident", [128, 128], F32, kind="ExternalInput")
    y_h = nc.dram_tensor("y_bounce", [DSL, B, T], F32)
    if debug:
        dlnx_h = nc.dram_tensor("dbg_lnx0", [128, CH], F32, kind="ExternalOutput")
        dxz_h = nc.dram_tensor("dbg_xzT0", [96, CH], F32, kind="ExternalOutput")
        ddt_h = nc.dram_tensor("dbg_dt0", [128, CH], F32, kind="ExternalOutput")
        da_h = nc.dram_tensor("dbg_a7", [128, T], F32, kind="ExternalOutput")
        dh_h = nc.dram_tensor("dbg_h6", [128, T], F32, kind="ExternalOutput")

    scr_dt = [nc.dram_tensor(f"scr_dt{c}", [B, 8, 32, TS], F32) for c in range(NCH)]
    scr_dtu = [nc.dram_tensor(f"scr_dtu{c}", [B, 8, 32, TS], F32) for c in range(NCH)]
    scr_bc = [nc.dram_tensor(f"scr_bc{c}", [2, DS, CH], F32) for c in range(NCH)]

    with ExitStack() as ctx:
        sb = lambda nm, shp, dt=F32: ctx.enter_context(nc.sbuf_tensor(nm, shp, dt))
        ps = lambda nm, shp: ctx.enter_context(nc.psum_tensor(nm, shp, F32))
        sem = lambda nm: semctx.enter_context(nc.semaphore(nm))

        xrow = [sb(f"xrow{i}", [128, D]) for i in range(2)]
        xslrow = [sb(f"xslrow{i}", [128, DSL]) for i in range(2)]
        xT = [sb(f"xT{k}", [128, CH]) for k in range(16)]
        uraw = [sb(f"uraw{k}", [128, CH]) for k in range(2)]
        bnbuf = sb("bnbuf", [128, 24])
        mvbuf = sb("mvbuf", [128, 2])
        vecol = sb("vecol", [128, 1])
        sdcol = sb("sdcol", [128, 1])
        rcol = sb("rcol", [128, 1])
        wx_sb = [sb(f"wx{k}", [128, 96]) for k in range(16)]
        wdt_sb = sb("wdt", [DR, DSL])
        bdt_sb = sb("bdt", [128, 2])
        negA_sb = sb("negA", [128, 32])
        negA2 = sb("negA2", [128, 32])
        dp_sb = sb("dp", [128, 2])
        osel_sb = ctx.enter_context(nc.sbuf_tensor("osel_sb", [128, 16 * 128], BF16))
        id_sb = sb("id_sb", [128, 128])
        xzT = sb("xzT", [96, CH])
        espb = sb("espb", [128, CH])
        dt_sb = [sb(f"dt_sb{k}", [128, CH]) for k in range(2)]
        dtu_sb = [sb(f"dtu_sb{k}", [128, CH]) for k in range(2)]
        bb_sb = [sb(f"bb_sb{b}", [128, TS]) for b in range(B)]
        cb_sb = [sb(f"cb_sb{b}", [128, TS]) for b in range(B)]
        GSZ = 4 * TS
        dtb = [sb(f"dtb{i}", [128, GSZ]) for i in range(2)]
        dtub = [sb(f"dtub{i}", [128, GSZ]) for i in range(2)]
        abuf = [sb(f"abuf{i}", [128, GSZ]) for i in range(2)]
        xbuf = sb("xbuf", [128, GSZ])
        hbuf = [sb("hbuf0", [128, GSZ])] * 2
        chbuf = ctx.enter_context(nc.sbuf_tensor("chbuf", [128, GSZ], BF16))
        hcar = sb("hcar", [128, 64])
        ysb = sb("ysb", [128, TS])

        ptr = [ps(f"ptr{i}", [128, 128]) for i in range(2)]
        pxz = [ps(f"pxz{i}", [96, TS]) for i in range(2)]
        pdt = [ps(f"pdt{i}", [128, TS]) for i in range(2)]
        py = [ps(f"py{i}", [128, TS]) for i in range(2)]

        s_ld = sem("s_ld")
        s_init = sem("s_init")
        s_x = [sem("s_x0"), sem("s_x1")]
        s_xsl = [sem("s_xsl0"), sem("s_xsl1")]
        s_bn = sem("s_bn")
        s_sd = sem("s_sd")
        s_ln = sem("s_ln")
        s_trq = sem("s_trq")
        s_evq = sem("s_evq")
        s_te = sem("s_te")
        s_ve = sem("s_ve")
        s_sc = sem("s_sc")
        s_scw = sem("s_scw")
        s_bc = sem("s_bc")
        s_bcu = sem("s_bcu")
        s_bcb = sem("s_bcb")
        s_bcc = sem("s_bcc")
        s_aex = sem("s_aex")
        s_dve = sem("s_dve")
        s_ymm = sem("s_ymm")
        s_yst = sem("s_yst")
        s_yw = sem("s_yw")
        s_dbg = sem("s_dbg")

        block = ctx.enter_context(nc.Block())

        @block.sync
        def _(sync):
            for h, t in [(wdt_h, wdt_sb), (bdt_h, bdt_sb), (alog_h, negA_sb),
                         (dp_h, dp_sb), (osel_h, osel_sb), (id_h, id_sb)]:
                sync.dma_start(out=t[:, :], in_=h.ap()).then_inc(s_ld, 16)
            for k in range(16):
                sync.dma_start(out=wx_sb[k][:, :],
                               in_=bass.AP(wx_h, k * 128 * 96, [[96, 128], [1, 96]])
                               ).then_inc(s_ld, 16)

        @block.scalar
        def _(scalar):
            scalar.wait_ge(s_ld, 16 * 22)
            scalar.activation(negA_sb[:, :], negA_sb[:, :], AF.Exp).then_inc(s_init, 1)

        @block.vector
        def _(vector):
            vector.wait_ge(s_init, 1)
            vector.memset(hcar[:, :], 0.0)
            vector.tensor_scalar(negA2[:, :], negA_sb[:, :], -1.0, None,
                                 AluOpType.mult).then_inc(s_init, 1)

        n_dbg = 0
        for c in range(NCH):
            for rt in range(8):
                gidx = 8 * c + rt
                s_i, r0 = rt // 4, TS * c + 128 * (rt % 4)
                col0 = s_i * TS + 128 * (rt % 4)

                @block.sync
                def _(sync, gidx=gidx, rt=rt, s_i=s_i, r0=r0):
                    if gidx >= 2:
                        sync.wait_ge(s_trq, 18 * (gidx - 1))
                    sync.dma_start(out=xrow[rt % 2][:, :],
                                   in_=bass.AP(x_h, (s_i * T + r0) * D,
                                               [[D, 128], [1, D]])).then_inc(s_x[gidx % 2], 16)
                    sync.dma_start(out=xslrow[rt % 2][:, :],
                                   in_=bass.AP(xsl_h, (s_i * T + r0) * DSL,
                                               [[DSL, 128], [1, DSL]])).then_inc(s_xsl[gidx % 2], 16)

                @block.vector
                def _(vector, gidx=gidx, rt=rt):
                    vector.wait_ge(s_x[gidx % 2], 16 * (gidx // 2 + 1))
                    for q in range(4):
                        vector.bn_stats(bnbuf[:, 6 * q:6 * (q + 1)],
                                        xrow[rt % 2][:, TS * q:TS * (q + 1)])
                    vector.drain()
                    vector.bn_aggr(mvbuf[:, :], bnbuf[:, :])
                    vector.drain()
                    vector.tensor_scalar(vecol[:, :], mvbuf[:, 1:2], EPS, None,
                                         AluOpType.add).then_inc(s_bn, 1)

                @block.scalar
                def _(scalar, gidx=gidx):
                    scalar.wait_ge(s_bn, gidx + 1)
                    scalar.activation(sdcol[:, :], vecol[:, :], AF.Ln)
                    scalar.drain()
                    scalar.activation(rcol[:, :], sdcol[:, :], AF.Exp,
                                      scale=-0.5).then_inc(s_sd, 1)

                @block.vector
                def _(vector, gidx=gidx, rt=rt):
                    vector.wait_ge(s_sd, gidx + 1)
                    vector.tensor_scalar(xrow[rt % 2][:, :], xrow[rt % 2][:, :],
                                         mvbuf[:, 0:1], rcol[:, :],
                                         AluOpType.subtract, AluOpType.mult)
                    vector.wait_ge(s_xsl[gidx % 2], 16 * (gidx // 2 + 1))
                    vector.tensor_scalar(xslrow[rt % 2][:, :], xslrow[rt % 2][:, :],
                                         mvbuf[:, 0:1], rcol[:, :],
                                         AluOpType.subtract, AluOpType.mult
                                         ).then_inc(s_ln, 1)

                @block.tensor
                def _(tensor, gidx=gidx, rt=rt):
                    tensor.wait_ge(s_ln, gidx + 1)
                    for j in range(18):
                        if gidx * 18 + j >= 2:
                            tensor.wait_ge(s_evq, gidx * 18 + j - 1)
                        src = (xrow[rt % 2][:, 128 * j:128 * (j + 1)] if j < 16
                               else xslrow[rt % 2][:, 128 * (j - 16):128 * (j - 15)])
                        tensor.transpose(ptr[j % 2][:, :], src,
                                         id_sb[:, :]).then_inc(s_trq, 1)

                @block.scalar
                def _(scalar, gidx=gidx, rt=rt, col0=col0, c=c):
                    for j in range(18):
                        scalar.wait_ge(s_trq, gidx * 18 + j + 1)
                        if j == 16 and c > 0 and rt == 0:
                            scalar.wait_ge(s_yst, 4 * c)
                        dst = xT[j] if j < 16 else uraw[j - 16]
                        scalar.copy(dst[:, col0:col0 + 128],
                                    ptr[j % 2][:, :]).then_inc(s_evq, 1)

            @block.tensor
            def _(tensor, c=c):
                tensor.wait_ge(s_evq, 144 * (c + 1))
                for half in range(2):
                    for k in range(16):
                        ins = tensor.matmul(pxz[half][:, :], wx_sb[k][:, 0:96],
                                            xT[k][:, TS * half:TS * (half + 1)],
                                            start=(k == 0), stop=(k == 15))
                        if k == 15:
                            ins.then_inc(s_te, 1)

            @block.vector
            def _(vector, c=c):
                vector.wait_ge(s_te, 4 * c + 2)
                if c > 0:
                    vector.wait_ge(s_scw, 144 * c)
                vector.tensor_copy(xzT[:, 0:TS], pxz[0][:, :])
                vector.tensor_copy(xzT[:, TS:CH], pxz[1][:, :]).then_inc(s_ve, 1)

            for dt_i in range(2):
                @block.tensor
                def _(tensor, c=c, dt_i=dt_i):
                    if dt_i == 0:
                        tensor.wait_ge(s_ve, 2 * c + 1)
                    else:
                        tensor.wait_ge(s_sc, 2 * c + 1)
                    for half in range(2):
                        ins = tensor.matmul(pdt[half][:, :],
                                            wdt_sb[0:64, 128 * dt_i:128 * (dt_i + 1)],
                                            xzT[0:64, TS * half:TS * (half + 1)],
                                            start=True, stop=True)
                    ins.then_inc(s_te, 1)

                @block.scalar
                def _(scalar, c=c, dt_i=dt_i):
                    scalar.wait_ge(s_te, 4 * c + 3 + dt_i)
                    if c > 0 and dt_i == 0:
                        scalar.wait_ge(s_scw, 144 * c)
                    for half in range(2):
                        sl = slice(TS * half, TS * (half + 1))
                        scalar.activation(espb[:, sl], pdt[half][:, :],
                                          AF.Exp, bias=bdt_sb[:, dt_i:dt_i + 1])
                        scalar.drain()
                        ins = scalar.activation(dt_sb[dt_i][:, sl], espb[:, sl],
                                                AF.Ln, bias=1.0)
                    ins.then_inc(s_sc, 1)

            @block.vector
            def _(vector, c=c):
                vector.wait_ge(s_sc, 2 * c + 2)
                if c > 0:
                    vector.wait_ge(s_scw, 144 * c)
                for dt_i in range(2):
                    ins = vector.tensor_tensor(dtu_sb[dt_i][:, :], dt_sb[dt_i][:, :],
                                               uraw[dt_i][:, :], AluOpType.mult)
                ins.then_inc(s_ve, 1)

            @block.gpsimd
            def _(g, c=c):
                g.wait_ge(s_ve, 2 * c + 2)
                for dt_i in range(2):
                    for bb in range(B):
                        doff = bb * (8 * 32 * TS) + dt_i * 16 * TS
                        dap = [[TS, 16], [32 * TS, 8], [1, TS]]
                        g.dma_start(out=bass.AP(scr_dt[c], doff, dap),
                                    in_=dt_sb[dt_i][:, bb * TS:(bb + 1) * TS]
                                    ).then_inc(s_scw, 16)
                        g.dma_start(out=bass.AP(scr_dtu[c], doff, dap),
                                    in_=dtu_sb[dt_i][:, bb * TS:(bb + 1) * TS]
                                    ).then_inc(s_scw, 16)
                g.dma_start(out=bass.AP(scr_bc[c], 0, [[CH, 32], [1, CH]]),
                            in_=xzT[64:96, :]).then_inc(s_scw, 16)

            for sg in range(16):
                b, de = sg // 8, sg % 8
                sgi = 16 * c + sg
                ygrp = 4 * c + sg // 4

                @block.gpsimd
                def _(g, b=b, de=de, sgi=sgi, c=c):
                    g.wait_ge(s_scw, 144 * (c + 1))
                    if sgi >= 2:
                        g.wait_ge(s_aex, sgi - 1)
                        g.wait_ge(s_dve, sgi - 1)
                    soff = b * (8 * 32 * TS) + de * 4 * TS
                    sap = [[32 * TS, 8], [0, 16], [1, 4 * TS]]
                    g.dma_start(out=dtb[sgi % 2][:, :],
                                in_=bass.AP(scr_dt[c], soff, sap)).then_inc(s_bc, 16)
                    g.dma_start(out=dtub[sgi % 2][:, :],
                                in_=bass.AP(scr_dtu[c], soff, sap)).then_inc(s_bcu, 16)
                    if de == 0:
                        g.dma_start(out=bb_sb[b][:, :],
                                    in_=bass.AP(scr_bc[c], b * TS,
                                                [[0, 8], [CH, 16], [1, TS]])
                                    ).then_inc(s_bcb, 16)
                        g.dma_start(out=cb_sb[b][:, :],
                                    in_=bass.AP(scr_bc[c], 16 * CH + b * TS,
                                                [[0, 8], [CH, 16], [1, TS]])
                                    ).then_inc(s_bcc, 16)

                @block.scalar
                def _(scalar, b=b, de=de, sgi=sgi, c=c, sg=sg):
                    scalar.wait_ge(s_bc, 16 * (sgi + 1))
                    if sgi == 0:
                        scalar.wait_ge(s_init, 2)
                    if sgi >= 2:
                        scalar.wait_ge(s_dve, sgi - 1)
                    for gp in range(4):
                        G = de * 4 + gp
                        ins = scalar.activation(abuf[sgi % 2][:, TS * gp:TS * (gp + 1)],
                                                dtb[sgi % 2][:, TS * gp:TS * (gp + 1)],
                                                AF.Exp, scale=negA2[:, G:G + 1])
                        if gp == 3:
                            ins.then_inc(s_aex, 1)

                @block.vector
                def _(vector, b=b, de=de, sgi=sgi, c=c, sg=sg):
                    vector.wait_ge(s_aex, sgi + 1)
                    vector.wait_ge(s_bcu, 16 * (sgi + 1))
                    vector.wait_ge(s_bcb, 16 * (2 * c + b + 1))
                    vector.wait_ge(s_bcc, 16 * (2 * c + b + 1))
                    if sgi >= 1:
                        vector.wait_ge(s_ymm, sgi)
                    for gp in range(4):
                        sl = slice(TS * gp, TS * (gp + 1))
                        vector.tensor_tensor(xbuf[:, sl], dtub[sgi % 2][:, sl],
                                             bb_sb[b][:, :], AluOpType.mult)
                    vector.drain()
                    for gp in range(4):
                        G = de * 4 + gp
                        sl = slice(TS * gp, TS * (gp + 1))
                        vector.tensor_tensor_scan(
                            hbuf[sg % 2][:, sl], abuf[sgi % 2][:, sl], xbuf[:, sl],
                            hcar[:, b * 32 + G:b * 32 + G + 1],
                            AluOpType.mult, AluOpType.add)
                    vector.drain()
                    for gp in range(4):
                        G = de * 4 + gp
                        vector.tensor_copy(hcar[:, b * 32 + G:b * 32 + G + 1],
                                           hbuf[sg % 2][:, TS * (gp + 1) - 1:TS * (gp + 1)])
                    vector.drain()
                    for gp in range(4):
                        sl = slice(TS * gp, TS * (gp + 1))
                        ins = vector.tensor_tensor(chbuf[:, sl], hbuf[sg % 2][:, sl],
                                                   cb_sb[b][:, :], AluOpType.mult)
                        if gp == 3:
                            ins.then_inc(s_dve, 1)

                @block.tensor
                def _(tensor, b=b, de=de, sgi=sgi, ygrp=ygrp, sg=sg):
                    tensor.wait_ge(s_dve, sgi + 1)
                    if de % 4 == 0 and ygrp >= 2:
                        tensor.wait_ge(s_yst, ygrp - 1)
                    for gp in range(4):
                        gl = (de % 4) * 4 + gp
                        ins = tensor.matmul(py[(sg // 4) % 2][:, :],
                                            osel_sb[:, 128 * gl:128 * (gl + 1)],
                                            chbuf[:, TS * gp:TS * (gp + 1)],
                                            start=(gl == 0), stop=(gl == 15))
                        if gp == 3:
                            ins.then_inc(s_ymm, 1)

                if de % 4 == 3:
                    dh = de // 4

                    @block.vector
                    def _(vector, b=b, dh=dh, ygrp=ygrp, sgi=sgi):
                        vector.wait_ge(s_ymm, sgi + 1)
                        if ygrp >= 1:
                            vector.wait_ge(s_yw, 16 * ygrp)
                        vector.scalar_tensor_tensor(
                            ysb[:, :], uraw[dh][:, b * TS:(b + 1) * TS],
                            dp_sb[:, dh:dh + 1], py[ygrp % 2][:, :],
                            AluOpType.mult, AluOpType.add).then_inc(s_yst, 1)

                    @block.sync
                    def _(sync, b=b, dh=dh, ygrp=ygrp, c=c):
                        sync.wait_ge(s_yst, ygrp + 1)
                        sync.dma_start(
                            out=bass.AP(y_h, (dh * 128) * B * T + b * T + TS * c,
                                        [[B * T, 128], [1, TS]]),
                            in_=ysb[:, :]).then_inc(s_yw, 16)

            if debug:
                @block.sync
                def _(sync, c=c):
                    sync.wait_ge(s_dve, 16 * c + 16)
                    if c == 0:
                        sync.dma_start(out=dlnx_h.ap(), in_=xT[0][:, :]).then_inc(s_dbg, 16)
                        sync.dma_start(out=dxz_h.ap(), in_=xzT[:, :]).then_inc(s_dbg, 16)
                        sync.dma_start(out=ddt_h.ap(), in_=dt_sb[0][:, :]).then_inc(s_dbg, 16)
                    sync.dma_start(out=bass.AP(da_h, TS * c, [[T, 128], [1, TS]]),
                                   in_=abuf[1][:, 0:TS]).then_inc(s_dbg, 16)
                    sync.dma_start(out=bass.AP(dh_h, TS * c, [[T, 128], [1, TS]]),
                                   in_=hbuf[0][:, 0:TS]).then_inc(s_dbg, 16)
                n_dbg += 5 if c == 0 else 2

    return y_h, s_yw



def osel_const():
    o = np.zeros((128, 16, 128), np.float32)
    for p in range(128):
        for gl in range(16):
            o[p, gl, gl * 8 + (p >> 4)] = 1.0
    import ml_dtypes
    return np.ascontiguousarray(o.reshape(128, 16 * 128)).astype(ml_dtypes.bfloat16)


def pp_layout(v_sl):
    return np.ascontiguousarray(v_sl.reshape(2, 128).T)


def alog_layout(a_sl):
    a = a_sl.reshape(32, 8, 16)
    a = a.transpose(0, 1, 2).reshape(32, 128)
    return np.ascontiguousarray(a.T)


def make_in_maps_a(d):
    x = np.ascontiguousarray(d["L0_attn_x"], dtype=np.float32)
    Wx = np.ascontiguousarray(d["L0_attn_Wx"], dtype=np.float32)
    Wdt = np.ascontiguousarray(d["L0_attn_Wdt"], dtype=np.float32)
    bdt = np.asarray(d["L0_attn_bdt"], dtype=np.float32)
    alog = np.asarray(d["L0_attn_A_log"], dtype=np.float32)
    Dp = np.asarray(d["L0_attn_Dp"], dtype=np.float32)
    osel = osel_const()
    ident = np.eye(128, dtype=np.float32)
    ims = []
    for r in range(8):
        sl = slice(DSL * r, DSL * (r + 1))
        ims.append({
            "x": x,
            "x_sl": np.ascontiguousarray(x[:, :, sl]),
            "wx": Wx,
            "wdt_sl": np.ascontiguousarray(Wdt[:, sl]),
            "bdt_pp": pp_layout(bdt[sl]),
            "alog_pp": alog_layout(alog[sl]),
            "dp_pp": pp_layout(Dp[sl]),
            "osel": osel,
            "ident": ident,
        })
    return ims





D, QKV, FFN, E = 2048, 3072, 4096, 8
NE = 8
STAGE = 99
RPC = 512
NQ, QC = 2, 512
HC = QC // 2
EPS = 1e-5
SC128 = float(1.0 / np.sqrt(128.0))


class Sem:
    def __init__(self, ctx, nc, name):
        self.h = ctx.enter_context(nc.semaphore(name))
        self.n = 0

    def inc(self, ins, k=1):
        ins.then_inc(self.h, k)
        self.n += k
        return ins


def emit_partb(nc, semctx, yag_h, s_cc):
    xr_h = nc.dram_tensor("x_rows", [2, RPC, D], F32, kind="ExternalInput")
    qkvT_h = nc.dram_tensor("qkvT", [D, QKV], F32, kind="ExternalInput")
    projT_h = nc.dram_tensor("projT", [D, D], F32, kind="ExternalInput")
    rw_h = nc.dram_tensor("router_w", [D, E], F32, kind="ExternalInput")
    w1_h = nc.dram_tensor("moe_w1", [E, D, FFN], F32, kind="ExternalInput")
    w2_h = nc.dram_tensor("moe_w2", [E, FFN, D], F32, kind="ExternalInput")
    ln1w_h = nc.dram_tensor("ln1w_pp", [128, 16], F32, kind="ExternalInput")
    ln1b_h = nc.dram_tensor("ln1b_pp", [128, 16], F32, kind="ExternalInput")
    ln2w_h = nc.dram_tensor("ln2w_pp", [128, 16], F32, kind="ExternalInput")
    ln2b_h = nc.dram_tensor("ln2b_pp", [128, 16], F32, kind="ExternalInput")
    id_h = nc.dram_tensor("ident2", [128, 128], F32, kind="ExternalInput")
    hsel_h = nc.dram_tensor("hsel", [128, 256], F32, kind="ExternalInput")
    out_h = nc.dram_tensor("out_r", [2, RPC, D], F32, kind="ExternalOutput")
    attw_d = nc.dram_tensor("attw_scr", [2, 16, HC], F32)
    gat_d = nc.dram_tensor("gat_scr", [E, QC], F32)

    with ExitStack() as ctx:
        sb = lambda nm, shp, dt=F32: ctx.enter_context(nc.sbuf_tensor(nm + "_pb", shp, dt))
        S = lambda nm: Sem(semctx, nc, nm + "_pb")

        xrow = [sb(f"xrow{i}", [128, D]) for i in range(2)]
        ybuf = [sb(f"ybuf{i}", [128, 16 * 128], BF16) for i in range(2)]
        x1b = sb("x1b", [128, 16 * QC], BF16)
        lnx1b = sb("lnx1b", [128, 16 * QC], BF16)
        mmin = sb("mmin", [128, 16 * QC], BF16)
        qb = sb("qb", [128, 24 * QC], BF16)
        yg = sb("yg", [128, 32 * QC], BF16)
        tmph = sb("tmph", [128, 16 * QC])
        gelb = [sb(f"gelb{i}", [128, QC], BF16) for i in range(2)]
        prodb = [sb(f"prodb{i}", [128, HC], BF16) for i in range(2)]
        wq = [sb(f"wq{i}", [128, 16 * 128], BF16) for i in range(2)]
        w2s = [sb("w2s0", [128, 32 * 128], BF16)] * 2
        wbb = sb("wbb", [128, 16 * HC], BF16)
        gbsb = [sb(f"gbsb{i}", [128, QC], BF16) for i in range(2)]
        ln1w = sb("ln1w", [128, 16]); ln1bb = sb("ln1bb", [128, 16])
        ln2w = sb("ln2w", [128, 16]); ln2bb = sb("ln2bb", [128, 16])
        rw_sb = sb("rw_sb", [128, 16 * E], BF16)
        id_sb = sb("id_sb", [128, 128])
        hsel_sb = sb("hsel_sb", [128, 256], BF16)
        onesb = sb("onesb", [128, 1], BF16)
        onesrow = sb("onesrow", [1, 128])
        att = sb("att", [16, 6 * HC])
        rows = sb("rows", [1, 3 * QC])
        gtok = sb("gtok", [128, 5 * E])
        gcol = sb("gcol", [128, 6])
        neg30 = sb("neg30", [128, E])
        gater = sb("gater", [E, QC])

        psn = lambda nm: ctx.enter_context(nc.psum_tensor(nm + '_pb', [128, QC], F32))
        pA = psn("pA"); pB = psn("pB"); pC = psn("pC"); pD = psn("pD")
        pE = psn("pE"); pF = psn("pF"); pG = psn("pG"); pH = psn("pH")

        s_ld = S("s_ld")
        s_ldg = S("s_ldg")
        s_xr = [S("s_xr0"), S("s_xr1")]
        s_yl = [S("s_yl0"), S("s_yl1")]
        s_wq = [S("s_wq0"), S("s_wq1")]
        s_w2 = [S("s_w20"), S("s_w21")]
        s_gb = [S("s_gb0"), S("s_gb1")]
        s_aw = S("s_aw")
        s_awg = S("s_awg")
        s_te = S("s_te")
        s_sc = S("s_sc")
        s_ve = S("s_ve")
        s_ow = S("s_ow")

        block = ctx.enter_context(nc.Block())

        @block.sync
        def _(sync):
            sync.wait_ge(s_cc.h if hasattr(s_cc, "h") else s_cc, 1)
            for h, t in [(ln1w_h, ln1w), (ln1b_h, ln1bb), (ln2w_h, ln2w),
                         (ln2b_h, ln2bb), (id_h, id_sb)]:
                s_ld.inc(sync.dma_start(out=t[:, :], in_=h.ap()), 16)

        @block.gpsimd
        def _(g):
            g.wait_ge(s_cc.h if hasattr(s_cc, "h") else s_cc, 1)

        @block.tensor
        def _(tensor):
            tensor.wait_ge(s_cc.h if hasattr(s_cc, "h") else s_cc, 1)

        @block.scalar
        def _(scalar):
            scalar.wait_ge(s_cc.h if hasattr(s_cc, "h") else s_cc, 1)

        @block.vector
        def _(vector):
            vector.wait_ge(s_cc.h if hasattr(s_cc, "h") else s_cc, 1)

        @block.gpsimd
        def _(g):
            s_ldg.inc(g.dma_start(out=hsel_sb[:, :], in_=hsel_h.ap()), 16)
            s_ldg.inc(g.dma_start(
                out=rw_sb[:, :],
                in_=bass.AP(rw_h, 0, [[E, 128], [128 * E, 16], [1, E]])), 16)

        @block.vector
        def _(vector):
            vector.wait_ge(s_ld.h, s_ld.n)
            vector.wait_ge(s_ldg.h, s_ldg.n)
            vector.memset(onesb[:, :], 1.0)
            vector.memset(onesrow[:, :], 1.0)
            s_ve.inc(vector.memset(neg30[:, :], -1e30))

        def stats_pair(src_bf, sq_dst):
            @block.vector
            def _(vector):
                vector.wait_ge(s_ve.h, s_ve.n)
                for k in range(16):
                    ins = vector.tensor_tensor(sq_dst[:, k * QC:(k + 1) * QC],
                                               src_bf[:, k * QC:(k + 1) * QC],
                                               src_bf[:, k * QC:(k + 1) * QC],
                                               AluOpType.mult)
                s_ve.inc(ins)

            @block.tensor
            def _(tensor):
                tensor.wait_ge(s_ve.h, s_ve.n)
                for k in range(16):
                    tensor.matmul(pE[0:1, :], onesb[:, :],
                                  src_bf[:, k * QC:(k + 1) * QC],
                                  start=(k == 0), stop=(k == 15))
                for k in range(16):
                    ins = tensor.matmul(pF[0:1, :], onesb[:, :],
                                        sq_dst[:, k * QC:(k + 1) * QC],
                                        start=(k == 0), stop=(k == 15))
                s_te.inc(ins)

            @block.vector
            def _(vector):
                vector.wait_ge(s_te.h, s_te.n)
                vector.tensor_scalar(rows[:, 0:QC], pE[0:1, :], 1.0 / D, None,
                                     AluOpType.mult)
                vector.tensor_scalar(rows[:, QC:2 * QC], pF[0:1, :], 1.0 / D, None,
                                     AluOpType.mult)
                vector.drain()
                vector.tensor_tensor(att[0:1, 0:QC], rows[:, 0:QC],
                                     rows[:, 0:QC], AluOpType.mult)
                vector.drain()
                vector.tensor_tensor(rows[:, QC:2 * QC], rows[:, QC:2 * QC],
                                     att[0:1, 0:QC], AluOpType.subtract)
                vector.drain()
                s_ve.inc(vector.tensor_scalar(rows[:, QC:2 * QC],
                                              rows[:, QC:2 * QC], EPS, None,
                                              AluOpType.add))

            @block.scalar
            def _(scalar):
                scalar.wait_ge(s_ve.h, s_ve.n)
                scalar.activation(att[0:1, 0:QC], rows[:, QC:2 * QC], AF.Ln)
                scalar.drain()
                s_sc.inc(scalar.activation(rows[:, 2 * QC:3 * QC], att[0:1, 0:QC],
                                           AF.Exp, scale=-0.5))

            @block.vector
            def _(vector):
                vector.wait_ge(s_sc.h, s_sc.n)
                s_ve.inc(vector.tensor_copy(gcol[0:1, 0:1], gcol[0:1, 0:1]))

            @block.tensor
            def _(tensor):
                tensor.wait_ge(s_ve.h, s_ve.n)
                tensor.matmul(pD[:, :], onesrow[0:1, :], rows[0:1, 0:QC],
                              start=True, stop=True)
                s_te.inc(tensor.matmul(pH[:, :], onesrow[0:1, :],
                                       rows[0:1, 2 * QC:3 * QC],
                                       start=True, stop=True))

        def mm_stream(w_dram, wtot, n_out, mov, ktiles, evac):
            te_after = {}
            ev_after = {}
            mov_ready = s_ve.n
            for o in range(n_out):
                @block.gpsimd
                def _(g, o=o):
                    if o >= 2:
                        g.wait_ge(s_te.h, te_after[o - 2])
                    s_wq[o % 2].inc(g.dma_start(
                        out=wq[o % 2][:, 0:ktiles * 128],
                        in_=bass.AP(w_dram, 128 * o,
                                    [[wtot, 128], [128 * wtot, ktiles], [1, 128]])), 16)

                @block.tensor
                def _(tensor, o=o, mov_ready=mov_ready):
                    tensor.wait_ge(s_wq[o % 2].h, s_wq[o % 2].n)
                    if o == 0:
                        tensor.wait_ge(s_ve.h, mov_ready)
                    if o >= 2:
                        tensor.wait_ge(*ev_after[o - 2])
                    pdst = pB if o % 2 == 0 else pC
                    for k in range(ktiles):
                        ins = tensor.matmul(pdst[:, :],
                                            wq[o % 2][:, 128 * k:128 * (k + 1)],
                                            mov[:, k * QC:(k + 1) * QC],
                                            start=(k == 0), stop=(k == ktiles - 1))
                    s_te.inc(ins)
                te_after[o] = s_te.n
                ev_after[o] = evac(o, pB if o % 2 == 0 else pC)

        def evac_qkv(o, psum):
            @block.scalar
            def _(scalar, o=o, psum=psum):
                scalar.wait_ge(s_te.h, s_te.n)
                s_sc.inc(scalar.activation(qb[:, o * QC:(o + 1) * QC], psum[:, :],
                                           AF.Copy))
            return (s_sc.h, s_sc.n)

        def evac_proj(o, psum):
            @block.vector
            def _(vector, o=o, psum=psum):
                vector.wait_ge(s_te.h, s_te.n)
                s_ve.inc(vector.tensor_tensor(tmph[:, o * QC:(o + 1) * QC],
                                              lnx1b[:, o * QC:(o + 1) * QC],
                                              psum[:, :], AluOpType.add))
            return (s_ve.h, s_ve.n)

        tr_cons = {}
        tr_sub = {}
        for q in range(NQ):

            for rt in range(4):
                gidx = 4 * q + rt
                s_i, r0 = rt // 2, HC * q + 128 * (rt % 2)
                colb = s_i * HC + 128 * (rt % 2)

                @block.sync
                def _(sync, gidx=gidx, rt=rt, s_i=s_i, r0=r0, q=q):
                    if gidx >= 2:
                        sync.wait_ge(s_ve.h, tr_cons[gidx - 2])
                    if q > 0 and rt < 2:
                        sync.wait_ge(s_ow.h, 16 * 4 * q)
                    s_xr[gidx % 2].inc(sync.dma_start(
                        out=xrow[rt % 2][:, :],
                        in_=bass.AP(xr_h, (s_i * RPC + r0) * D, [[D, 128], [1, D]])), 16)

                @block.gpsimd
                def _(g, gidx=gidx, rt=rt, s_i=s_i, r0=r0, q=q):
                    if gidx >= 2:
                        g.wait_ge(s_ve.h, tr_cons[gidx - 2])
                    for arm in nc.switch_core_id(engines=[g], n=128):
                        rank = int(arm.logical) % 8
                        g.dma_start(
                            out=ybuf[rt % 2][:, :],
                            in_=bass.AP(yag_h, s_i * T + RPC * rank + r0,
                                        [[2 * T, 128], [128 * 2 * T, 16], [1, 128]])
                        ).then_inc(s_yl[gidx % 2].h, 16)
                    s_yl[gidx % 2].n += 16

                for sub in range(4):
                    @block.tensor
                    def _(tensor, gidx=gidx, rt=rt, sub=sub):
                        if sub == 0:
                            tensor.wait_ge(s_xr[gidx % 2].h, s_xr[gidx % 2].n)
                        if gidx * 4 + sub >= 1:
                            tensor.wait_ge(s_ve.h, tr_sub[gidx * 4 + sub - 1])
                        for jj in range(4):
                            j = 4 * sub + jj
                            ins = tensor.transpose(
                                pA[:, 128 * jj:128 * jj + 128],
                                xrow[rt % 2][:, 128 * j:128 * (j + 1)], id_sb[:, :])
                        s_te.inc(ins)

                    @block.vector
                    def _(vector, gidx=gidx, rt=rt, colb=colb, q=q, sub=sub):
                        vector.wait_ge(s_te.h, s_te.n)
                        if sub == 0:
                            vector.wait_ge(s_yl[gidx % 2].h, s_yl[gidx % 2].n)
                        for jj in range(4):
                            j = 4 * sub + jj
                            ins = vector.tensor_tensor(
                                x1b[:, j * QC + colb:j * QC + colb + 128],
                                pA[:, 128 * jj:128 * jj + 128],
                                ybuf[rt % 2][:, j * 128:(j + 1) * 128],
                                AluOpType.add)
                        s_ve.inc(ins)
                    tr_sub[gidx * 4 + sub] = s_ve.n
                tr_cons[gidx] = s_ve.n

            stats_pair(x1b, mmin)

            @block.vector
            def _(vector):
                vector.wait_ge(s_te.h, s_te.n)
                for k in range(16):
                    sl = slice(k * QC, (k + 1) * QC)
                    vector.tensor_tensor(lnx1b[:, sl], x1b[:, sl], pD[:, :],
                                         AluOpType.subtract)
                vector.drain()
                for k in range(16):
                    sl = slice(k * QC, (k + 1) * QC)
                    ins = vector.tensor_tensor(lnx1b[:, sl], lnx1b[:, sl], pH[:, :],
                                               AluOpType.mult)
                s_ve.inc(ins)

            stats_pair(lnx1b, mmin)

            @block.vector
            def _(vector):
                vector.wait_ge(s_te.h, s_te.n)
                for k in range(16):
                    sl = slice(k * QC, (k + 1) * QC)
                    vector.tensor_tensor(mmin[:, sl], lnx1b[:, sl], pD[:, :],
                                         AluOpType.subtract)
                vector.drain()
                for k in range(16):
                    sl = slice(k * QC, (k + 1) * QC)
                    vector.tensor_tensor(mmin[:, sl], mmin[:, sl], pH[:, :],
                                         AluOpType.mult)
                vector.drain()
                for k in range(16):
                    sl = slice(k * QC, (k + 1) * QC)
                    ins = vector.tensor_scalar(mmin[:, sl], mmin[:, sl],
                                               ln1w[:, k:k + 1], ln1bb[:, k:k + 1],
                                               AluOpType.mult, AluOpType.add)
                s_ve.inc(ins)

            if STAGE < 5: break
            mm_stream(qkvT_h, QKV, 24, mmin, 16, evac_qkv)

            if STAGE < 6: break
            for h in range(16):
                g = h // 4
                tq, tk, tv = 6 * g + h % 4, 6 * g + 4, 6 * g + 5

                @block.vector
                def _(vector, h=h, tq=tq, tk=tk):
                    vector.wait_ge(s_sc.h, s_sc.n)
                    if h >= 1:
                        vector.wait_ge(s_te.h, s_te.n)
                    vector.tensor_tensor(prodb[0][:, :], qb[:, tq * QC + HC:(tq + 1) * QC],
                                         qb[:, tk * QC:tk * QC + HC], AluOpType.mult)
                    s_ve.inc(vector.tensor_tensor(
                        prodb[1][:, :], qb[:, tq * QC + HC:(tq + 1) * QC],
                        qb[:, tk * QC + HC:(tk + 1) * QC], AluOpType.mult))

                @block.tensor
                def _(tensor, h=h):
                    tensor.wait_ge(s_ve.h, s_ve.n)
                    tensor.matmul(pE[0:16, 0:HC], hsel_sb[:, 16 * h:16 * (h + 1)],
                                  prodb[0][:, :], start=(h == 0), stop=(h == 15))
                    ins = tensor.matmul(pF[0:16, 0:HC], hsel_sb[:, 16 * h:16 * (h + 1)],
                                        prodb[1][:, :], start=(h == 0), stop=(h == 15))
                    s_te.inc(ins)

            @block.vector
            def _(vector):
                vector.wait_ge(s_te.h, s_te.n)
                vector.tensor_copy(att[:, 0:HC], pE[0:16, 0:HC])
                vector.drain()
                vector.tensor_tensor(att[:, 5 * HC:6 * HC], att[:, 0:HC],
                                     pF[0:16, 0:HC], AluOpType.max)
                vector.drain()
                vector.tensor_tensor(att[:, HC:2 * HC], att[:, 0:HC],
                                     att[:, 5 * HC:6 * HC], AluOpType.subtract)
                s_ve.inc(vector.tensor_tensor(att[:, 2 * HC:3 * HC], pF[0:16, 0:HC],
                                              att[:, 5 * HC:6 * HC],
                                              AluOpType.subtract))

            @block.scalar
            def _(scalar):
                scalar.wait_ge(s_ve.h, s_ve.n)
                scalar.activation(att[:, 3 * HC:4 * HC], att[:, HC:2 * HC], AF.Exp)
                s_sc.inc(scalar.activation(att[:, 4 * HC:5 * HC], att[:, 2 * HC:3 * HC],
                                           AF.Exp))

            @block.vector
            def _(vector):
                vector.wait_ge(s_sc.h, s_sc.n)
                vector.tensor_tensor(att[:, 5 * HC:6 * HC], att[:, 3 * HC:4 * HC],
                                     att[:, 4 * HC:5 * HC], AluOpType.add)
                vector.drain()
                vector.reciprocal(att[:, 0:HC], att[:, 5 * HC:6 * HC])
                vector.drain()
                vector.tensor_tensor(att[:, HC:2 * HC], att[:, 3 * HC:4 * HC],
                                     att[:, 0:HC], AluOpType.mult)
                s_ve.inc(vector.tensor_tensor(att[:, 2 * HC:3 * HC],
                                              att[:, 4 * HC:5 * HC],
                                              att[:, 0:HC], AluOpType.mult))

            @block.sync
            def _(sync):
                sync.wait_ge(s_ve.h, s_ve.n)
                s_aw.inc(sync.dma_start(
                    out=bass.AP(attw_d, 0, [[HC, 16], [1, HC]]),
                    in_=att[:, HC:2 * HC]), 16)
                s_aw.inc(sync.dma_start(
                    out=bass.AP(attw_d, 16 * HC, [[HC, 16], [1, HC]]),
                    in_=att[:, 2 * HC:3 * HC]), 16)

            @block.gpsimd
            def _(g, q=q):
                g.wait_ge(s_aw.h, s_aw.n)
                if q > 0:
                    g.wait_ge(s_ve.h, s_ve.n)
                s_awg.inc(g.dma_start(
                    out=wbb[:, :],
                    in_=bass.AP(attw_d, 0, [[0, 128], [HC, 16], [1, HC]])), 16)

            @block.vector
            def _(vector, q=q):
                vector.wait_ge(s_awg.h, s_awg.n)
                for h in range(16):
                    g2 = h // 4
                    tv = 6 * g2 + 5
                    ins = vector.tensor_tensor(
                        yg[:, h * QC:h * QC + HC], wbb[:, h * HC:(h + 1) * HC],
                        qb[:, tv * QC:tv * QC + HC], AluOpType.mult)
                s_ve.inc(ins)

            @block.gpsimd
            def _(g):
                g.wait_ge(s_ve.h, s_ve.n)
                s_awg.inc(g.dma_start(
                    out=wbb[:, :],
                    in_=bass.AP(attw_d, 16 * HC, [[0, 128], [HC, 16], [1, HC]])), 16)

            @block.vector
            def _(vector):
                vector.wait_ge(s_awg.h, s_awg.n)
                for h in range(16):
                    g2 = h // 4
                    tv = 6 * g2 + 5
                    vector.tensor_tensor(gelb[0][:, 0:HC], wbb[:, h * HC:(h + 1) * HC],
                                         qb[:, tv * QC + HC:(tv + 1) * QC],
                                         AluOpType.mult)
                    vector.drain()
                    vector.tensor_tensor(yg[:, h * QC + HC:(h + 1) * QC],
                                         yg[:, h * QC:h * QC + HC],
                                         gelb[0][:, 0:HC], AluOpType.add)
                    vector.drain()
                    ins = vector.tensor_copy(yg[:, h * QC:h * QC + HC],
                                             qb[:, tv * QC:tv * QC + HC])
                s_ve.inc(ins)

            if STAGE < 7: break
            mm_stream(projT_h, D, 16, yg, 16, evac_proj)

            @block.vector
            def _(vector):
                vector.wait_ge(s_ve.h, s_ve.n)
                for k in range(16):
                    ins = vector.tensor_copy(mmin[:, k * QC:(k + 1) * QC],
                                             tmph[:, k * QC:(k + 1) * QC])
                s_ve.inc(ins)
            stats_pair(mmin, qb)

            @block.vector
            def _(vector):
                vector.wait_ge(s_te.h, s_te.n)
                for k in range(16):
                    sl = slice(k * QC, (k + 1) * QC)
                    vector.tensor_tensor(tmph[:, sl], tmph[:, sl], x1b[:, sl],
                                         AluOpType.add)
                vector.drain()
                for k in range(16):
                    sl = slice(k * QC, (k + 1) * QC)
                    vector.tensor_tensor(mmin[:, sl], mmin[:, sl], pD[:, :],
                                         AluOpType.subtract)
                vector.drain()
                for k in range(16):
                    sl = slice(k * QC, (k + 1) * QC)
                    vector.tensor_tensor(mmin[:, sl], mmin[:, sl], pH[:, :],
                                         AluOpType.mult)
                vector.drain()
                for k in range(16):
                    sl = slice(k * QC, (k + 1) * QC)
                    ins = vector.tensor_scalar(mmin[:, sl], mmin[:, sl],
                                               ln2w[:, k:k + 1], ln2bb[:, k:k + 1],
                                               AluOpType.mult, AluOpType.add)
                s_ve.inc(ins)

            if STAGE < 9: break
            for tg in range(4):
                @block.tensor
                def _(tensor, tg=tg):
                    tensor.wait_ge(s_ve.h, s_ve.n)
                    for k in range(16):
                        ins = tensor.matmul(pG[0:128, 8 * tg:8 * (tg + 1)],
                                            mmin[:, k * QC + 128 * tg:k * QC + 128 * (tg + 1)],
                                            rw_sb[:, 8 * k:8 * (k + 1)],
                                            start=(k == 0), stop=(k == 15))
                    s_te.inc(ins)

                @block.vector
                def _(vector, tg=tg):
                    vector.wait_ge(s_te.h, s_te.n)
                    pgb = pG[0:128, 8 * tg:8 * (tg + 1)]
                    vector.tensor_copy(gtok[:, 0:E], pgb)
                    vector.drain()
                    vector.tensor_reduce(gcol[:, 0:1], gtok[:, 0:E], AX.X,
                                         AluOpType.max)
                    vector.drain()
                    vector.tensor_scalar(gtok[:, E:2 * E], gtok[:, 0:E],
                                         gcol[:, 0:1], None, AluOpType.is_ge)
                    vector.drain()
                    vector.scalar_tensor_tensor(gtok[:, 2 * E:3 * E],
                                                gtok[:, E:2 * E], -1e30,
                                                gtok[:, 0:E],
                                                AluOpType.mult, AluOpType.add)
                    vector.drain()
                    vector.tensor_reduce(gcol[:, 1:2], gtok[:, 2 * E:3 * E], AX.X,
                                         AluOpType.max)
                    s_ve.inc(vector.tensor_scalar(gcol[:, 4:5], gcol[:, 0:1], -1.0,
                                                  None, AluOpType.mult))

                @block.scalar
                def _(scalar, tg=tg):
                    scalar.wait_ge(s_ve.h, s_ve.n)
                    s_sc.inc(scalar.activation(gtok[:, 2 * E:3 * E], gtok[:, 0:E],
                                               AF.Exp, bias=gcol[:, 4:5]))

                @block.vector
                def _(vector, tg=tg):
                    vector.wait_ge(s_sc.h, s_sc.n)
                    vector.tensor_scalar(gtok[:, E:2 * E], gtok[:, 0:E],
                                         gcol[:, 1:2], None, AluOpType.is_ge)
                    vector.drain()
                    vector.tensor_tensor(gtok[:, 3 * E:4 * E], gtok[:, 2 * E:3 * E],
                                         gtok[:, E:2 * E], AluOpType.mult)
                    vector.drain()
                    vector.tensor_reduce(gcol[:, 2:3], gtok[:, 3 * E:4 * E], AX.X,
                                         AluOpType.add)
                    vector.drain()
                    vector.reciprocal(gcol[:, 3:4], gcol[:, 2:3])
                    vector.drain()
                    s_ve.inc(vector.tensor_scalar(gtok[:, 4 * E:5 * E],
                                                  gtok[:, 3 * E:4 * E],
                                                  gcol[:, 3:4], None, AluOpType.mult))

                @block.tensor
                def _(tensor, tg=tg):
                    tensor.wait_ge(s_ve.h, s_ve.n)
                    s_te.inc(tensor.transpose(pH[0:8, 128 * tg:128 * (tg + 1)],
                                              gtok[:, 4 * E:5 * E], id_sb[:, :]))

                @block.vector
                def _(vector, tg=tg):
                    vector.wait_ge(s_te.h, s_te.n)
                    s_ve.inc(vector.tensor_copy(gater[:, 128 * tg:128 * (tg + 1)],
                                                pH[0:8, 128 * tg:128 * (tg + 1)]))

            @block.sync
            def _(sync):
                sync.wait_ge(s_ve.h, s_ve.n)
                s_aw.inc(sync.dma_start(out=gat_d.ap(), in_=gater[:, :]), 16)

            if STAGE < 10: break
            te_w1 = {}
            sc_gel = {}
            ve_y1g = {}
            xf_ready = s_ve.n
            for e in range(NE):
                @block.gpsimd
                def _(g, e=e, q=q):
                    g.wait_ge(s_aw.h, s_aw.n)
                    if e >= 2:
                        g.wait_ge(s_ve.h, ve_y1g[e - 2])
                    elif q > 0 and e < 2:
                        g.wait_ge(s_ve.h, s_ve.n)
                    s_gb[e % 2].inc(g.dma_start(
                        out=gbsb[e % 2][:, :],
                        in_=bass.AP(gat_d, e * QC, [[0, 128], [1, QC]])), 16)

                for o2 in range(32):
                    @block.gpsimd
                    def _(g, e=e, o2=o2):
                        if (e, o2) != (0, 0) and (e * 32 + o2) >= 2:
                            g.wait_ge(s_te.h, te_w1[e * 32 + o2 - 2])
                        s_wq[o2 % 2].inc(g.dma_start(
                            out=wq[o2 % 2][:, :],
                            in_=bass.AP(w1_h, e * D * FFN + 128 * o2,
                                        [[FFN, 128], [128 * FFN, 16], [1, 128]])), 16)

                    @block.tensor
                    def _(tensor, e=e, o2=o2, xf_ready=xf_ready):
                        tensor.wait_ge(s_wq[o2 % 2].h, s_wq[o2 % 2].n)
                        gi = e * 32 + o2
                        if gi == 0:
                            tensor.wait_ge(s_ve.h, xf_ready)
                        if gi >= 2:
                            tensor.wait_ge(s_sc.h, sc_gel[gi - 2])
                        pdst = pB if o2 % 2 == 0 else pC
                        for k in range(16):
                            ins = tensor.matmul(pdst[:, :],
                                                wq[o2 % 2][:, 128 * k:128 * (k + 1)],
                                                mmin[:, k * QC:(k + 1) * QC],
                                                start=(k == 0), stop=(k == 15))
                        s_te.inc(ins)
                    te_w1[e * 32 + o2] = s_te.n

                    @block.scalar
                    def _(scalar, e=e, o2=o2):
                        scalar.wait_ge(s_te.h, s_te.n)
                        gi = e * 32 + o2
                        if gi >= 2:
                            scalar.wait_ge(s_ve.h, ve_y1g.get(("o2", gi - 2), 0))
                        pdst = pB if o2 % 2 == 0 else pC
                        s_sc.inc(scalar.activation(gelb[o2 % 2][:, :], pdst[:, :],
                                                   AF.Gelu))
                    sc_gel[e * 32 + o2] = s_sc.n

                    @block.vector
                    def _(vector, e=e, o2=o2):
                        vector.wait_ge(s_sc.h, s_sc.n)
                        vector.wait_ge(s_gb[e % 2].h, s_gb[e % 2].n)
                        s_ve.inc(vector.tensor_tensor(
                            yg[:, o2 * QC:(o2 + 1) * QC], gelb[o2 % 2][:, :],
                            gbsb[e % 2][:, :], AluOpType.mult))
                    ve_y1g[("o2", e * 32 + o2)] = s_ve.n

                ve_y1g[e] = s_ve.n

                te_w2 = {}
                ve_acc = {}
                for o in range(16):
                    @block.gpsimd
                    def _(g, e=e, o=o):
                        if o >= 1:
                            g.wait_ge(s_te.h, te_w2[o - 1])
                        elif e >= 1:
                            g.wait_ge(s_te.h, s_te.n)
                        s_w2[o % 2].inc(g.dma_start(
                            out=w2s[o % 2][:, :],
                            in_=bass.AP(w2_h, e * FFN * D + 128 * o,
                                        [[D, 128], [128 * D, 32], [1, 128]])), 16)

                    @block.tensor
                    def _(tensor, e=e, o=o):
                        tensor.wait_ge(s_w2[o % 2].h, s_w2[o % 2].n)
                        tensor.wait_ge(s_ve.h, ve_y1g[e])
                        if o >= 2:
                            tensor.wait_ge(s_ve.h, ve_acc[o - 2])
                        pdst = pB if o % 2 == 0 else pC
                        for k in range(32):
                            ins = tensor.matmul(pdst[:, :],
                                                w2s[o % 2][:, 128 * k:128 * (k + 1)],
                                                yg[:, k * QC:(k + 1) * QC],
                                                start=(k == 0), stop=(k == 31))
                        s_te.inc(ins)
                    te_w2[o] = s_te.n

                    @block.vector
                    def _(vector, e=e, o=o):
                        vector.wait_ge(s_te.h, s_te.n)
                        s_ve.inc(vector.tensor_tensor(
                            tmph[:, o * QC:(o + 1) * QC], tmph[:, o * QC:(o + 1) * QC],
                            (pB if o % 2 == 0 else pC)[:, :], AluOpType.add))
                    ve_acc[o] = s_ve.n

            if STAGE < 11: break
            osub = {}
            for tg2 in range(4):
                for sub in range(4):
                    @block.tensor
                    def _(tensor, tg2=tg2, q=q, sub=sub):
                        if sub == 0:
                            tensor.wait_ge(s_ve.h, s_ve.n)
                            if tg2 >= 2:
                                tensor.wait_ge(s_ow.h, 16 * (4 * q + tg2 - 1))
                        if True:
                            tensor.wait_ge(s_sc.h, osub.get((tg2, sub - 1), s_sc.n if sub == 0 else 0))
                        for kk in range(4):
                            k = 4 * sub + kk
                            ins = tensor.transpose(
                                pA[:, 128 * kk:128 * kk + 128],
                                tmph[:, k * QC + 128 * tg2:k * QC + 128 * (tg2 + 1)],
                                id_sb[:, :])
                        s_te.inc(ins)

                    @block.scalar
                    def _(scalar, tg2=tg2, sub=sub):
                        scalar.wait_ge(s_te.h, s_te.n)
                        for kk in range(4):
                            k = 4 * sub + kk
                            ins = scalar.copy(xrow[tg2 % 2][:, 128 * k:128 * (k + 1)],
                                              pA[:, 128 * kk:128 * kk + 128])
                        s_sc.inc(ins)
                    osub[(tg2, sub)] = s_sc.n

                @block.sync
                def _(sync, tg2=tg2, q=q):
                    sync.wait_ge(s_sc.h, s_sc.n)
                    s_i = tg2 // 2
                    r0 = HC * q + 128 * (tg2 % 2)
                    s_ow.inc(sync.dma_start(
                        out=bass.AP(out_h, (s_i * RPC + r0) * D, [[D, 128], [1, D]]),
                        in_=xrow[tg2 % 2][:, :]), 16)

        @block.sync
        def _(sync):
            sync.wait_ge(s_ow.h, s_ow.n)

    return None



def pp16(v):
    return np.ascontiguousarray(np.asarray(v, np.float32).reshape(16, 128).T)


def hsel_const():
    h = np.zeros((128, 16, 16), np.float32)
    for hh in range(16):
        h[:, hh, hh] = SC128
    return np.ascontiguousarray(h.reshape(128, 256))


def make_in_maps_b(d, y_full):
    qkvT = np.ascontiguousarray(np.asarray(d["L0_ffn_qkv_weight"], np.float32).T)
    projT = np.ascontiguousarray(np.asarray(d["L0_ffn_proj_weight"], np.float32).T)
    ims = []
    x = np.asarray(d["L0_attn_x"], np.float32)
    for r in range(8):
        rs = slice(RPC * r, RPC * (r + 1))
        ims.append({
            "x_rows": np.ascontiguousarray(x[:, rs, :]),
            "qkvT": qkvT,
            "projT": projT,
            "router_w": np.ascontiguousarray(d["L0_ffn_router_weight"], dtype=np.float32),
            "moe_w1": np.ascontiguousarray(d["L0_ffn_moe_w1"], dtype=np.float32),
            "moe_w2": np.ascontiguousarray(d["L0_ffn_moe_w2"], dtype=np.float32),
            "ln1w_pp": pp16(d["L0_ffn_ln1_weight"]),
            "ln1b_pp": pp16(d["L0_ffn_ln1_bias"]),
            "ln2w_pp": pp16(d["L0_ffn_ln2_weight"]),
            "ln2b_pp": pp16(d["L0_ffn_ln2_bias"]),
            "ident2": np.eye(128, dtype=np.float32),
            "hsel": hsel_const(),
        })
    return ims




def build_fused():
    nc = bass.Bass("TRN2", target_bir_lowering=False, detect_race_conditions=False)
    with ExitStack() as semctx:
        s_cc = semctx.enter_context(nc.semaphore("s_cc"))
        y_h, s_yw = emit_parta(nc, semctx)
        yag_h = nc.dram_tensor("yag", [D, B, T], F32)
        with nc.Block() as blk:
            @blk.gpsimd
            def _(g):
                g.wait_ge(s_yw, 16 * 4 * NCH)
                g.collective_compute(
                    "AllGather", AluOpType.bypass,
                    replica_groups=[list(range(8))],
                    ins=[y_h.ap().opt()], outs=[yag_h.ap().opt()],
                ).then_inc(s_cc, 1)
        emit_partb(nc, semctx, yag_h, s_cc)
    return nc


def kernel(**inputs):
    d = inputs
    ims_a = make_in_maps_a(d)
    ims_b = make_in_maps_b(d, None)
    ims = [{**a, **b} for a, b in zip(ims_a, ims_b)]
    nc = build_fused()
    res = run_bass_kernel_spmd(nc, ims, core_ids=list(range(8)))
    out = np.concatenate([np.asarray(res.results[r]["out_r"]).reshape(2, RPC, D)
                          for r in range(8)], axis=1)
    return np.ascontiguousarray(out, dtype=np.float32)


# version 05 (final)
# speedup vs baseline: 1.3885x; 1.2535x over previous
from contextlib import ExitStack
import numpy as np
import concourse.bass as bass
import concourse.mybir as mybir
from concourse.alu_op_type import AluOpType
from concourse.bass_utils import run_bass_kernel_spmd

F32 = mybir.dt.float32
BF16 = mybir.dt.bfloat16
AF = mybir.ActivationFunctionType
AX = mybir.AxisListType

B, T, D, DS, DR = 2, 4096, 2048, 16, 64
DSL = D // 8
NCH, TS = 8, 512
CH = B * TS
EPS = 1e-5


def bc_target(c, sg):
    full = c * 36
    within = 0
    for s in range(sg + 1):
        within += 2 + (2 if s % 8 == 0 else 0)
    return 16 * (full + within)


def emit_parta(nc, semctx, debug=False):
    x_h = nc.dram_tensor("x", [B, T, D], F32, kind="ExternalInput")
    xsl_h = nc.dram_tensor("x_sl", [B, T, DSL], F32, kind="ExternalInput")
    wx_h = nc.dram_tensor("wx", [D, 96], F32, kind="ExternalInput")
    wdt_h = nc.dram_tensor("wdt_sl", [DR, DSL], F32, kind="ExternalInput")
    bdt_h = nc.dram_tensor("bdt_pp", [128, 2], F32, kind="ExternalInput")
    alog_h = nc.dram_tensor("alog_pp", [128, 32], F32, kind="ExternalInput")
    dp_h = nc.dram_tensor("dp_pp", [128, 2], F32, kind="ExternalInput")
    osel_h = nc.dram_tensor("osel", [128, 16 * 128], BF16, kind="ExternalInput")
    id_h = nc.dram_tensor("ident", [128, 128], F32, kind="ExternalInput")
    y_h = nc.dram_tensor("y_bounce", [DSL, B, T], F32)
    if debug:
        dlnx_h = nc.dram_tensor("dbg_lnx0", [128, CH], F32, kind="ExternalOutput")
        dxz_h = nc.dram_tensor("dbg_xzT0", [96, CH], F32, kind="ExternalOutput")
        ddt_h = nc.dram_tensor("dbg_dt0", [128, CH], F32, kind="ExternalOutput")
        da_h = nc.dram_tensor("dbg_a7", [128, T], F32, kind="ExternalOutput")
        dh_h = nc.dram_tensor("dbg_h6", [128, T], F32, kind="ExternalOutput")

    scr_dt = [nc.dram_tensor(f"scr_dt{c}", [B, 8, 32, TS], F32) for c in range(NCH)]
    scr_dtu = [nc.dram_tensor(f"scr_dtu{c}", [B, 8, 32, TS], F32) for c in range(NCH)]
    scr_bc = [nc.dram_tensor(f"scr_bc{c}", [2, DS, CH], F32) for c in range(NCH)]

    with ExitStack() as ctx:
        sb = lambda nm, shp, dt=F32: ctx.enter_context(nc.sbuf_tensor(nm, shp, dt))
        ps = lambda nm, shp: ctx.enter_context(nc.psum_tensor(nm, shp, F32))
        sem = lambda nm: semctx.enter_context(nc.semaphore(nm))

        xrow = [sb(f"xrow{i}", [128, D]) for i in range(2)]
        xslrow = [sb(f"xslrow{i}", [128, DSL]) for i in range(2)]
        xT = [sb(f"xT{k}", [128, CH]) for k in range(16)]
        uraw = [sb(f"uraw{k}", [128, CH]) for k in range(2)]
        bnbuf = sb("bnbuf", [128, 24])
        mvbuf = sb("mvbuf", [128, 2])
        vecol = sb("vecol", [128, 1])
        sdcol = sb("sdcol", [128, 1])
        rcol = sb("rcol", [128, 1])
        wx_sb = [sb(f"wx{k}", [128, 96]) for k in range(16)]
        wdt_sb = sb("wdt", [DR, DSL])
        bdt_sb = sb("bdt", [128, 2])
        negA_sb = sb("negA", [128, 32])
        negA2 = sb("negA2", [128, 32])
        dp_sb = sb("dp", [128, 2])
        osel_sb = ctx.enter_context(nc.sbuf_tensor("osel_sb", [128, 16 * 128], BF16))
        id_sb = sb("id_sb", [128, 128])
        xzT = sb("xzT", [96, CH])
        espb = sb("espb", [128, CH])
        dt_sb = [sb(f"dt_sb{k}", [128, CH]) for k in range(2)]
        dtu_sb = [sb(f"dtu_sb{k}", [128, CH]) for k in range(2)]
        bb_sb = [sb(f"bb_sb{b}", [128, TS]) for b in range(B)]
        cb_sb = [sb(f"cb_sb{b}", [128, TS]) for b in range(B)]
        GSZ = 4 * TS
        dtb = [sb(f"dtb{i}", [128, GSZ]) for i in range(2)]
        dtub = [sb(f"dtub{i}", [128, GSZ]) for i in range(2)]
        abuf = [sb(f"abuf{i}", [128, GSZ]) for i in range(2)]
        xbuf = sb("xbuf", [128, GSZ])
        hbuf = [sb("hbuf0", [128, GSZ])] * 2
        chbuf = ctx.enter_context(nc.sbuf_tensor("chbuf", [128, GSZ], BF16))
        hcar = sb("hcar", [128, 64])
        ysb = sb("ysb", [128, TS])

        ptr = [ps(f"ptr{i}", [128, 128]) for i in range(2)]
        pxz = [ps(f"pxz{i}", [96, TS]) for i in range(2)]
        pdt = [ps(f"pdt{i}", [128, TS]) for i in range(2)]
        py = [ps(f"py{i}", [128, TS]) for i in range(2)]

        s_ld = sem("s_ld")
        s_init = sem("s_init")
        s_x = [sem("s_x0"), sem("s_x1")]
        s_xsl = [sem("s_xsl0"), sem("s_xsl1")]
        s_bn = sem("s_bn")
        s_sd = sem("s_sd")
        s_ln = sem("s_ln")
        s_trq = sem("s_trq")
        s_evq = sem("s_evq")
        s_te = sem("s_te")
        s_ve = sem("s_ve")
        s_sc = sem("s_sc")
        s_scw = sem("s_scw")
        s_bc = sem("s_bc")
        s_bcu = sem("s_bcu")
        s_bcb = sem("s_bcb")
        s_bcc = sem("s_bcc")
        s_aex = sem("s_aex")
        s_dve = sem("s_dve")
        s_ymm = sem("s_ymm")
        s_yst = sem("s_yst")
        s_yw = sem("s_yw")
        s_dbg = sem("s_dbg")

        block = ctx.enter_context(nc.Block())

        @block.sync
        def _(sync):
            for h, t in [(wdt_h, wdt_sb), (bdt_h, bdt_sb), (alog_h, negA_sb),
                         (dp_h, dp_sb), (osel_h, osel_sb), (id_h, id_sb)]:
                sync.dma_start(out=t[:, :], in_=h.ap()).then_inc(s_ld, 16)
            for k in range(16):
                sync.dma_start(out=wx_sb[k][:, :],
                               in_=bass.AP(wx_h, k * 128 * 96, [[96, 128], [1, 96]])
                               ).then_inc(s_ld, 16)

        @block.scalar
        def _(scalar):
            scalar.wait_ge(s_ld, 16 * 22)
            scalar.activation(negA_sb[:, :], negA_sb[:, :], AF.Exp).then_inc(s_init, 1)

        @block.vector
        def _(vector):
            vector.wait_ge(s_init, 1)
            vector.memset(hcar[:, :], 0.0)
            vector.tensor_scalar(negA2[:, :], negA_sb[:, :], -1.0, None,
                                 AluOpType.mult).then_inc(s_init, 1)

        n_dbg = 0
        for c in range(NCH):
            for rt in range(8):
                gidx = 8 * c + rt
                s_i, r0 = rt // 4, TS * c + 128 * (rt % 4)
                col0 = s_i * TS + 128 * (rt % 4)

                @block.sync
                def _(sync, gidx=gidx, rt=rt, s_i=s_i, r0=r0):
                    if gidx >= 2:
                        sync.wait_ge(s_trq, 18 * (gidx - 1))
                    sync.dma_start(out=xrow[rt % 2][:, :],
                                   in_=bass.AP(x_h, (s_i * T + r0) * D,
                                               [[D, 128], [1, D]])).then_inc(s_x[gidx % 2], 16)
                    sync.dma_start(out=xslrow[rt % 2][:, :],
                                   in_=bass.AP(xsl_h, (s_i * T + r0) * DSL,
                                               [[DSL, 128], [1, DSL]])).then_inc(s_xsl[gidx % 2], 16)

                @block.vector
                def _(vector, gidx=gidx, rt=rt):
                    vector.wait_ge(s_x[gidx % 2], 16 * (gidx // 2 + 1))
                    for q in range(4):
                        vector.bn_stats(bnbuf[:, 6 * q:6 * (q + 1)],
                                        xrow[rt % 2][:, TS * q:TS * (q + 1)])
                    vector.drain()
                    vector.bn_aggr(mvbuf[:, :], bnbuf[:, :])
                    vector.drain()
                    vector.tensor_scalar(vecol[:, :], mvbuf[:, 1:2], EPS, None,
                                         AluOpType.add).then_inc(s_bn, 1)

                @block.scalar
                def _(scalar, gidx=gidx):
                    scalar.wait_ge(s_bn, gidx + 1)
                    scalar.activation(sdcol[:, :], vecol[:, :], AF.Ln)
                    scalar.drain()
                    scalar.activation(rcol[:, :], sdcol[:, :], AF.Exp,
                                      scale=-0.5).then_inc(s_sd, 1)

                @block.vector
                def _(vector, gidx=gidx, rt=rt):
                    vector.wait_ge(s_sd, gidx + 1)
                    vector.tensor_scalar(xrow[rt % 2][:, :], xrow[rt % 2][:, :],
                                         mvbuf[:, 0:1], rcol[:, :],
                                         AluOpType.subtract, AluOpType.mult)
                    vector.wait_ge(s_xsl[gidx % 2], 16 * (gidx // 2 + 1))
                    vector.tensor_scalar(xslrow[rt % 2][:, :], xslrow[rt % 2][:, :],
                                         mvbuf[:, 0:1], rcol[:, :],
                                         AluOpType.subtract, AluOpType.mult
                                         ).then_inc(s_ln, 1)

                @block.tensor
                def _(tensor, gidx=gidx, rt=rt):
                    tensor.wait_ge(s_ln, gidx + 1)
                    for j in range(18):
                        if gidx * 18 + j >= 2:
                            tensor.wait_ge(s_evq, gidx * 18 + j - 1)
                        src = (xrow[rt % 2][:, 128 * j:128 * (j + 1)] if j < 16
                               else xslrow[rt % 2][:, 128 * (j - 16):128 * (j - 15)])
                        tensor.transpose(ptr[j % 2][:, :], src,
                                         id_sb[:, :]).then_inc(s_trq, 1)

                @block.scalar
                def _(scalar, gidx=gidx, rt=rt, col0=col0, c=c):
                    for j in range(18):
                        scalar.wait_ge(s_trq, gidx * 18 + j + 1)
                        if j == 16 and c > 0 and rt == 0:
                            scalar.wait_ge(s_yst, 4 * c)
                        dst = xT[j] if j < 16 else uraw[j - 16]
                        scalar.copy(dst[:, col0:col0 + 128],
                                    ptr[j % 2][:, :]).then_inc(s_evq, 1)

            @block.tensor
            def _(tensor, c=c):
                tensor.wait_ge(s_evq, 144 * (c + 1))
                for half in range(2):
                    for k in range(16):
                        ins = tensor.matmul(pxz[half][:, :], wx_sb[k][:, 0:96],
                                            xT[k][:, TS * half:TS * (half + 1)],
                                            start=(k == 0), stop=(k == 15))
                        if k == 15:
                            ins.then_inc(s_te, 1)

            @block.vector
            def _(vector, c=c):
                vector.wait_ge(s_te, 4 * c + 2)
                if c > 0:
                    vector.wait_ge(s_scw, 144 * c)
                vector.tensor_copy(xzT[:, 0:TS], pxz[0][:, :])
                vector.tensor_copy(xzT[:, TS:CH], pxz[1][:, :]).then_inc(s_ve, 1)

            for dt_i in range(2):
                @block.tensor
                def _(tensor, c=c, dt_i=dt_i):
                    if dt_i == 0:
                        tensor.wait_ge(s_ve, 2 * c + 1)
                    else:
                        tensor.wait_ge(s_sc, 2 * c + 1)
                    for half in range(2):
                        ins = tensor.matmul(pdt[half][:, :],
                                            wdt_sb[0:64, 128 * dt_i:128 * (dt_i + 1)],
                                            xzT[0:64, TS * half:TS * (half + 1)],
                                            start=True, stop=True)
                    ins.then_inc(s_te, 1)

                @block.scalar
                def _(scalar, c=c, dt_i=dt_i):
                    scalar.wait_ge(s_te, 4 * c + 3 + dt_i)
                    if c > 0 and dt_i == 0:
                        scalar.wait_ge(s_scw, 144 * c)
                    for half in range(2):
                        sl = slice(TS * half, TS * (half + 1))
                        scalar.activation(espb[:, sl], pdt[half][:, :],
                                          AF.Exp, bias=bdt_sb[:, dt_i:dt_i + 1])
                        scalar.drain()
                        ins = scalar.activation(dt_sb[dt_i][:, sl], espb[:, sl],
                                                AF.Ln, bias=1.0)
                    ins.then_inc(s_sc, 1)

            @block.vector
            def _(vector, c=c):
                vector.wait_ge(s_sc, 2 * c + 2)
                if c > 0:
                    vector.wait_ge(s_scw, 144 * c)
                for dt_i in range(2):
                    ins = vector.tensor_tensor(dtu_sb[dt_i][:, :], dt_sb[dt_i][:, :],
                                               uraw[dt_i][:, :], AluOpType.mult)
                ins.then_inc(s_ve, 1)

            @block.gpsimd
            def _(g, c=c):
                g.wait_ge(s_ve, 2 * c + 2)
                for dt_i in range(2):
                    for bb in range(B):
                        doff = bb * (8 * 32 * TS) + dt_i * 16 * TS
                        dap = [[TS, 16], [32 * TS, 8], [1, TS]]
                        g.dma_start(out=bass.AP(scr_dt[c], doff, dap),
                                    in_=dt_sb[dt_i][:, bb * TS:(bb + 1) * TS]
                                    ).then_inc(s_scw, 16)
                        g.dma_start(out=bass.AP(scr_dtu[c], doff, dap),
                                    in_=dtu_sb[dt_i][:, bb * TS:(bb + 1) * TS]
                                    ).then_inc(s_scw, 16)
                g.dma_start(out=bass.AP(scr_bc[c], 0, [[CH, 32], [1, CH]]),
                            in_=xzT[64:96, :]).then_inc(s_scw, 16)

            for sg in range(16):
                b, de = sg // 8, sg % 8
                sgi = 16 * c + sg
                ygrp = 4 * c + sg // 4

                @block.gpsimd
                def _(g, b=b, de=de, sgi=sgi, c=c):
                    g.wait_ge(s_scw, 144 * (c + 1))
                    if sgi >= 2:
                        g.wait_ge(s_aex, sgi - 1)
                        g.wait_ge(s_dve, sgi - 1)
                    soff = b * (8 * 32 * TS) + de * 4 * TS
                    sap = [[32 * TS, 8], [0, 16], [1, 4 * TS]]
                    g.dma_start(out=dtb[sgi % 2][:, :],
                                in_=bass.AP(scr_dt[c], soff, sap)).then_inc(s_bc, 16)
                    g.dma_start(out=dtub[sgi % 2][:, :],
                                in_=bass.AP(scr_dtu[c], soff, sap)).then_inc(s_bcu, 16)
                    if de == 0:
                        g.dma_start(out=bb_sb[b][:, :],
                                    in_=bass.AP(scr_bc[c], b * TS,
                                                [[0, 8], [CH, 16], [1, TS]])
                                    ).then_inc(s_bcb, 16)
                        g.dma_start(out=cb_sb[b][:, :],
                                    in_=bass.AP(scr_bc[c], 16 * CH + b * TS,
                                                [[0, 8], [CH, 16], [1, TS]])
                                    ).then_inc(s_bcc, 16)

                @block.scalar
                def _(scalar, b=b, de=de, sgi=sgi, c=c, sg=sg):
                    scalar.wait_ge(s_bc, 16 * (sgi + 1))
                    if sgi == 0:
                        scalar.wait_ge(s_init, 2)
                    if sgi >= 2:
                        scalar.wait_ge(s_dve, sgi - 1)
                    for gp in range(4):
                        G = de * 4 + gp
                        ins = scalar.activation(abuf[sgi % 2][:, TS * gp:TS * (gp + 1)],
                                                dtb[sgi % 2][:, TS * gp:TS * (gp + 1)],
                                                AF.Exp, scale=negA2[:, G:G + 1])
                        if gp == 3:
                            ins.then_inc(s_aex, 1)

                @block.vector
                def _(vector, b=b, de=de, sgi=sgi, c=c, sg=sg):
                    vector.wait_ge(s_aex, sgi + 1)
                    vector.wait_ge(s_bcu, 16 * (sgi + 1))
                    vector.wait_ge(s_bcb, 16 * (2 * c + b + 1))
                    vector.wait_ge(s_bcc, 16 * (2 * c + b + 1))
                    if sgi >= 1:
                        vector.wait_ge(s_ymm, sgi)
                    for gp in range(4):
                        sl = slice(TS * gp, TS * (gp + 1))
                        vector.tensor_tensor(xbuf[:, sl], dtub[sgi % 2][:, sl],
                                             bb_sb[b][:, :], AluOpType.mult)
                    vector.drain()
                    for gp in range(4):
                        G = de * 4 + gp
                        sl = slice(TS * gp, TS * (gp + 1))
                        vector.tensor_tensor_scan(
                            hbuf[sg % 2][:, sl], abuf[sgi % 2][:, sl], xbuf[:, sl],
                            hcar[:, b * 32 + G:b * 32 + G + 1],
                            AluOpType.mult, AluOpType.add)
                    vector.drain()
                    for gp in range(4):
                        G = de * 4 + gp
                        vector.tensor_copy(hcar[:, b * 32 + G:b * 32 + G + 1],
                                           hbuf[sg % 2][:, TS * (gp + 1) - 1:TS * (gp + 1)])
                    vector.drain()
                    for gp in range(4):
                        sl = slice(TS * gp, TS * (gp + 1))
                        ins = vector.tensor_tensor(chbuf[:, sl], hbuf[sg % 2][:, sl],
                                                   cb_sb[b][:, :], AluOpType.mult)
                        if gp == 3:
                            ins.then_inc(s_dve, 1)

                @block.tensor
                def _(tensor, b=b, de=de, sgi=sgi, ygrp=ygrp, sg=sg):
                    tensor.wait_ge(s_dve, sgi + 1)
                    if de % 4 == 0 and ygrp >= 2:
                        tensor.wait_ge(s_yst, ygrp - 1)
                    for gp in range(4):
                        gl = (de % 4) * 4 + gp
                        ins = tensor.matmul(py[(sg // 4) % 2][:, :],
                                            osel_sb[:, 128 * gl:128 * (gl + 1)],
                                            chbuf[:, TS * gp:TS * (gp + 1)],
                                            start=(gl == 0), stop=(gl == 15))
                        if gp == 3:
                            ins.then_inc(s_ymm, 1)

                if de % 4 == 3:
                    dh = de // 4

                    @block.vector
                    def _(vector, b=b, dh=dh, ygrp=ygrp, sgi=sgi):
                        vector.wait_ge(s_ymm, sgi + 1)
                        if ygrp >= 1:
                            vector.wait_ge(s_yw, 16 * ygrp)
                        vector.scalar_tensor_tensor(
                            ysb[:, :], uraw[dh][:, b * TS:(b + 1) * TS],
                            dp_sb[:, dh:dh + 1], py[ygrp % 2][:, :],
                            AluOpType.mult, AluOpType.add).then_inc(s_yst, 1)

                    @block.sync
                    def _(sync, b=b, dh=dh, ygrp=ygrp, c=c):
                        sync.wait_ge(s_yst, ygrp + 1)
                        sync.dma_start(
                            out=bass.AP(y_h, (dh * 128) * B * T + b * T + TS * c,
                                        [[B * T, 128], [1, TS]]),
                            in_=ysb[:, :]).then_inc(s_yw, 16)

            if debug:
                @block.sync
                def _(sync, c=c):
                    sync.wait_ge(s_dve, 16 * c + 16)
                    if c == 0:
                        sync.dma_start(out=dlnx_h.ap(), in_=xT[0][:, :]).then_inc(s_dbg, 16)
                        sync.dma_start(out=dxz_h.ap(), in_=xzT[:, :]).then_inc(s_dbg, 16)
                        sync.dma_start(out=ddt_h.ap(), in_=dt_sb[0][:, :]).then_inc(s_dbg, 16)
                    sync.dma_start(out=bass.AP(da_h, TS * c, [[T, 128], [1, TS]]),
                                   in_=abuf[1][:, 0:TS]).then_inc(s_dbg, 16)
                    sync.dma_start(out=bass.AP(dh_h, TS * c, [[T, 128], [1, TS]]),
                                   in_=hbuf[0][:, 0:TS]).then_inc(s_dbg, 16)
                n_dbg += 5 if c == 0 else 2

    return y_h, s_yw



def osel_const():
    o = np.zeros((128, 16, 128), np.float32)
    for p in range(128):
        for gl in range(16):
            o[p, gl, gl * 8 + (p >> 4)] = 1.0
    import ml_dtypes
    return np.ascontiguousarray(o.reshape(128, 16 * 128)).astype(ml_dtypes.bfloat16)


def pp_layout(v_sl):
    return np.ascontiguousarray(v_sl.reshape(2, 128).T)


def alog_layout(a_sl):
    a = a_sl.reshape(32, 8, 16)
    a = a.transpose(0, 1, 2).reshape(32, 128)
    return np.ascontiguousarray(a.T)


def make_in_maps_a(d):
    x = np.ascontiguousarray(d["L0_attn_x"], dtype=np.float32)
    Wx = np.ascontiguousarray(d["L0_attn_Wx"], dtype=np.float32)
    Wdt = np.ascontiguousarray(d["L0_attn_Wdt"], dtype=np.float32)
    bdt = np.asarray(d["L0_attn_bdt"], dtype=np.float32)
    alog = np.asarray(d["L0_attn_A_log"], dtype=np.float32)
    Dp = np.asarray(d["L0_attn_Dp"], dtype=np.float32)
    osel = osel_const()
    ident = np.eye(128, dtype=np.float32)
    ims = []
    for r in range(8):
        sl = slice(DSL * r, DSL * (r + 1))
        ims.append({
            "x": x,
            "x_sl": np.ascontiguousarray(x[:, :, sl]),
            "wx": Wx,
            "wdt_sl": np.ascontiguousarray(Wdt[:, sl]),
            "bdt_pp": pp_layout(bdt[sl]),
            "alog_pp": alog_layout(alog[sl]),
            "dp_pp": pp_layout(Dp[sl]),
            "osel": osel,
            "ident": ident,
        })
    return ims





D, QKV, FFN, E = 2048, 3072, 4096, 8
NE = 8
STAGE = 99
RPC = 512
NQ, QC = 2, 512
HC = QC // 2
EPS = 1e-5
SC128 = float(1.0 / np.sqrt(128.0))


class Sem:
    def __init__(self, ctx, nc, name):
        self.h = ctx.enter_context(nc.semaphore(name))
        self.n = 0

    def inc(self, ins, k=1):
        ins.then_inc(self.h, k)
        self.n += k
        return ins


def emit_partb(nc, semctx, yag_h, s_cc):
    xr_h = nc.dram_tensor("x_rows", [2, RPC, D], F32, kind="ExternalInput")
    qkvT_h = nc.dram_tensor("qkvT", [D, QKV], F32, kind="ExternalInput")
    projT_h = nc.dram_tensor("projT", [D, D], F32, kind="ExternalInput")
    rw_h = nc.dram_tensor("router_w", [D, E], F32, kind="ExternalInput")
    w1_h = nc.dram_tensor("moe_w1", [E, D, FFN], F32, kind="ExternalInput")
    w2_h = nc.dram_tensor("moe_w2", [E, FFN, D], F32, kind="ExternalInput")
    ln1w_h = nc.dram_tensor("ln1w_pp", [128, 16], F32, kind="ExternalInput")
    ln1b_h = nc.dram_tensor("ln1b_pp", [128, 16], F32, kind="ExternalInput")
    ln2w_h = nc.dram_tensor("ln2w_pp", [128, 16], F32, kind="ExternalInput")
    ln2b_h = nc.dram_tensor("ln2b_pp", [128, 16], F32, kind="ExternalInput")
    id_h = nc.dram_tensor("ident2", [128, 128], F32, kind="ExternalInput")
    hsel_h = nc.dram_tensor("hsel", [128, 256], F32, kind="ExternalInput")
    out_h = nc.dram_tensor("out_r", [2, RPC, D], F32, kind="ExternalOutput")
    attw_d = nc.dram_tensor("attw_scr", [2, 16, HC], F32)
    gat_d = nc.dram_tensor("gat_scr", [E, QC], F32)

    with ExitStack() as ctx:
        sb = lambda nm, shp, dt=F32: ctx.enter_context(nc.sbuf_tensor(nm + "_pb", shp, dt))
        S = lambda nm: Sem(semctx, nc, nm + "_pb")

        xrow = [sb(f"xrow{i}", [128, D]) for i in range(2)]
        ybuf = [sb(f"ybuf{i}", [128, 16 * 128], BF16) for i in range(2)]
        x1b = sb("x1b", [128, 16 * QC], BF16)
        lnx1b = sb("lnx1b", [128, 16 * QC], BF16)
        mmin = sb("mmin", [128, 16 * QC], BF16)
        qb = sb("qb", [128, 24 * QC], BF16)
        yg = sb("yg", [128, 32 * QC], BF16)
        tmph = sb("tmph", [128, 16 * QC])
        gelb = [sb(f"gelb{i}", [128, QC], BF16) for i in range(4)]
        prodb = [sb(f"prodb{i}", [128, HC], BF16) for i in range(2)]
        wq = [sb(f"wq{i}", [128, 16 * 128], BF16) for i in range(2)]
        wbb = sb("wbb", [128, 16 * HC], BF16)
        gbsb = [sb(f"gbsb{i}", [128, QC], BF16) for i in range(2)]
        ln1w = sb("ln1w", [128, 16]); ln1bb = sb("ln1bb", [128, 16])
        ln2w = sb("ln2w", [128, 16]); ln2bb = sb("ln2bb", [128, 16])
        rw_sb = sb("rw_sb", [128, 16 * E], BF16)
        id_sb = sb("id_sb", [128, 128])
        hsel_sb = sb("hsel_sb", [128, 256], BF16)
        onesb = sb("onesb", [128, 1], BF16)
        onesrow = sb("onesrow", [1, 128])
        att = sb("att", [16, 6 * HC])
        rows = sb("rows", [1, 3 * QC])
        gtok = sb("gtok", [128, 5 * E])
        gcol = sb("gcol", [128, 6])
        neg30 = sb("neg30", [128, E])
        gater = sb("gater", [E, QC])

        psn = lambda nm: ctx.enter_context(nc.psum_tensor(nm + '_pb', [128, QC], F32))
        pA = psn("pA"); pB = psn("pB"); pC = psn("pC"); pD = psn("pD")
        pE = psn("pE"); pF = psn("pF"); pG = psn("pG"); pH = psn("pH")

        s_ld = S("s_ld")
        s_ldg = S("s_ldg")
        s_xr = [S("s_xr0"), S("s_xr1")]
        s_yl = [S("s_yl0"), S("s_yl1")]
        s_wq = [S("s_wq0"), S("s_wq1")]
        s_wp = [S("s_wp0"), S("s_wp1")]
        s_w2 = [S("s_w20"), S("s_w21")]
        s_gb = [S("s_gb0"), S("s_gb1")]
        s_aw = S("s_aw")
        s_awg = S("s_awg")
        s_te = S("s_te")
        s_sc = S("s_sc")
        s_ve = S("s_ve")
        s_ow = S("s_ow")

        block = ctx.enter_context(nc.Block())

        @block.sync
        def _(sync):
            sync.wait_ge(s_cc.h if hasattr(s_cc, "h") else s_cc, 1)
            for h, t in [(ln1w_h, ln1w), (ln1b_h, ln1bb), (ln2w_h, ln2w),
                         (ln2b_h, ln2bb), (id_h, id_sb)]:
                s_ld.inc(sync.dma_start(out=t[:, :], in_=h.ap()), 16)

        @block.gpsimd
        def _(g):
            g.wait_ge(s_cc.h if hasattr(s_cc, "h") else s_cc, 1)

        @block.tensor
        def _(tensor):
            tensor.wait_ge(s_cc.h if hasattr(s_cc, "h") else s_cc, 1)

        @block.scalar
        def _(scalar):
            scalar.wait_ge(s_cc.h if hasattr(s_cc, "h") else s_cc, 1)

        @block.vector
        def _(vector):
            vector.wait_ge(s_cc.h if hasattr(s_cc, "h") else s_cc, 1)

        @block.gpsimd
        def _(g):
            s_ldg.inc(g.dma_start(out=hsel_sb[:, :], in_=hsel_h.ap()), 16)
            s_ldg.inc(g.dma_start(
                out=rw_sb[:, :],
                in_=bass.AP(rw_h, 0, [[E, 128], [128 * E, 16], [1, E]])), 16)

        @block.vector
        def _(vector):
            vector.wait_ge(s_ld.h, s_ld.n)
            vector.wait_ge(s_ldg.h, s_ldg.n)
            vector.memset(onesb[:, :], 1.0)
            vector.memset(onesrow[:, :], 1.0)
            s_ve.inc(vector.memset(neg30[:, :], -1e30))

        def stats_pair(src_bf, sq_dst):
            @block.vector
            def _(vector):
                vector.wait_ge(s_ve.h, s_ve.n)
                for k in range(16):
                    ins = vector.tensor_tensor(sq_dst[:, k * QC:(k + 1) * QC],
                                               src_bf[:, k * QC:(k + 1) * QC],
                                               src_bf[:, k * QC:(k + 1) * QC],
                                               AluOpType.mult)
                s_ve.inc(ins)

            @block.tensor
            def _(tensor):
                tensor.wait_ge(s_ve.h, s_ve.n)
                for k in range(16):
                    tensor.matmul(pE[0:1, :], onesb[:, :],
                                  src_bf[:, k * QC:(k + 1) * QC],
                                  start=(k == 0), stop=(k == 15))
                for k in range(16):
                    ins = tensor.matmul(pF[0:1, :], onesb[:, :],
                                        sq_dst[:, k * QC:(k + 1) * QC],
                                        start=(k == 0), stop=(k == 15))
                s_te.inc(ins)

            @block.vector
            def _(vector):
                vector.wait_ge(s_te.h, s_te.n)
                vector.tensor_scalar(rows[:, 0:QC], pE[0:1, :], 1.0 / D, None,
                                     AluOpType.mult)
                vector.tensor_scalar(rows[:, QC:2 * QC], pF[0:1, :], 1.0 / D, None,
                                     AluOpType.mult)
                vector.drain()
                vector.tensor_tensor(att[0:1, 0:QC], rows[:, 0:QC],
                                     rows[:, 0:QC], AluOpType.mult)
                vector.drain()
                vector.tensor_tensor(rows[:, QC:2 * QC], rows[:, QC:2 * QC],
                                     att[0:1, 0:QC], AluOpType.subtract)
                vector.drain()
                s_ve.inc(vector.tensor_scalar(rows[:, QC:2 * QC],
                                              rows[:, QC:2 * QC], EPS, None,
                                              AluOpType.add))

            @block.scalar
            def _(scalar):
                scalar.wait_ge(s_ve.h, s_ve.n)
                scalar.activation(att[0:1, 0:QC], rows[:, QC:2 * QC], AF.Ln)
                scalar.drain()
                s_sc.inc(scalar.activation(rows[:, 2 * QC:3 * QC], att[0:1, 0:QC],
                                           AF.Exp, scale=-0.5))

            @block.vector
            def _(vector):
                vector.wait_ge(s_sc.h, s_sc.n)
                s_ve.inc(vector.tensor_copy(gcol[0:1, 0:1], gcol[0:1, 0:1]))

            @block.tensor
            def _(tensor):
                tensor.wait_ge(s_ve.h, s_ve.n)
                tensor.matmul(pD[:, :], onesrow[0:1, :], rows[0:1, 0:QC],
                              start=True, stop=True)
                s_te.inc(tensor.matmul(pH[:, :], onesrow[0:1, :],
                                       rows[0:1, 2 * QC:3 * QC],
                                       start=True, stop=True))

        def mm_stream(w_dram, wtot, n_out, mov, ktiles, evac):
            te_after = {}
            ev_after = {}
            mov_ready = s_ve.n
            for o in range(n_out):
                @block.gpsimd
                def _(g, o=o):
                    if o >= 2:
                        g.wait_ge(s_te.h, te_after[o - 2])
                    s_wq[o % 2].inc(g.dma_start(
                        out=wq[o % 2][:, 0:ktiles * 128],
                        in_=bass.AP(w_dram, 128 * o,
                                    [[wtot, 128], [128 * wtot, ktiles], [1, 128]])), 16)

                @block.tensor
                def _(tensor, o=o, mov_ready=mov_ready):
                    tensor.wait_ge(s_wq[o % 2].h, s_wq[o % 2].n)
                    if o == 0:
                        tensor.wait_ge(s_ve.h, mov_ready)
                    if o >= 2:
                        tensor.wait_ge(*ev_after[o - 2])
                    pdst = pB if o % 2 == 0 else pC
                    for k in range(ktiles):
                        ins = tensor.matmul(pdst[:, :],
                                            wq[o % 2][:, 128 * k:128 * (k + 1)],
                                            mov[:, k * QC:(k + 1) * QC],
                                            start=(k == 0), stop=(k == ktiles - 1))
                    s_te.inc(ins)
                te_after[o] = s_te.n
                ev_after[o] = evac(o, pB if o % 2 == 0 else pC)

        def evac_qkv(o, psum):
            @block.scalar
            def _(scalar, o=o, psum=psum):
                scalar.wait_ge(s_te.h, s_te.n)
                s_sc.inc(scalar.activation(qb[:, o * QC:(o + 1) * QC], psum[:, :],
                                           AF.Copy))
            return (s_sc.h, s_sc.n)

        def evac_proj(o, psum):
            @block.vector
            def _(vector, o=o, psum=psum):
                vector.wait_ge(s_te.h, s_te.n)
                s_ve.inc(vector.tensor_tensor(tmph[:, o * QC:(o + 1) * QC],
                                              lnx1b[:, o * QC:(o + 1) * QC],
                                              psum[:, :], AluOpType.add))
            return (s_ve.h, s_ve.n)

        tr_cons = {}
        tr_sub = {}
        te_w2d = {}
        for q in range(NQ):

            for rt in range(4):
                gidx = 4 * q + rt
                s_i, r0 = rt // 2, HC * q + 128 * (rt % 2)
                colb = s_i * HC + 128 * (rt % 2)

                @block.sync
                def _(sync, gidx=gidx, rt=rt, s_i=s_i, r0=r0, q=q):
                    if gidx >= 2:
                        sync.wait_ge(s_ve.h, tr_cons[gidx - 2])
                    if q > 0 and rt < 2:
                        sync.wait_ge(s_ow.h, 16 * 4 * q)
                    s_xr[gidx % 2].inc(sync.dma_start(
                        out=xrow[rt % 2][:, :],
                        in_=bass.AP(xr_h, (s_i * RPC + r0) * D, [[D, 128], [1, D]])), 16)

                @block.gpsimd
                def _(g, gidx=gidx, rt=rt, s_i=s_i, r0=r0, q=q):
                    if gidx >= 2:
                        g.wait_ge(s_ve.h, tr_cons[gidx - 2])
                    for arm in nc.switch_core_id(engines=[g], n=128):
                        rank = int(arm.logical) % 8
                        g.dma_start(
                            out=ybuf[rt % 2][:, :],
                            in_=bass.AP(yag_h, s_i * T + RPC * rank + r0,
                                        [[2 * T, 128], [128 * 2 * T, 16], [1, 128]])
                        ).then_inc(s_yl[gidx % 2].h, 16)
                    s_yl[gidx % 2].n += 16

                for sub in range(4):
                    @block.tensor
                    def _(tensor, gidx=gidx, rt=rt, sub=sub):
                        if sub == 0:
                            tensor.wait_ge(s_xr[gidx % 2].h, s_xr[gidx % 2].n)
                        if gidx * 4 + sub >= 1:
                            tensor.wait_ge(s_ve.h, tr_sub[gidx * 4 + sub - 1])
                        for jj in range(4):
                            j = 4 * sub + jj
                            ins = tensor.transpose(
                                pA[:, 128 * jj:128 * jj + 128],
                                xrow[rt % 2][:, 128 * j:128 * (j + 1)], id_sb[:, :])
                        s_te.inc(ins)

                    @block.vector
                    def _(vector, gidx=gidx, rt=rt, colb=colb, q=q, sub=sub):
                        vector.wait_ge(s_te.h, s_te.n)
                        if sub == 0:
                            vector.wait_ge(s_yl[gidx % 2].h, s_yl[gidx % 2].n)
                        for jj in range(4):
                            j = 4 * sub + jj
                            ins = vector.tensor_tensor(
                                x1b[:, j * QC + colb:j * QC + colb + 128],
                                pA[:, 128 * jj:128 * jj + 128],
                                ybuf[rt % 2][:, j * 128:(j + 1) * 128],
                                AluOpType.add)
                        s_ve.inc(ins)
                    tr_sub[gidx * 4 + sub] = s_ve.n
                tr_cons[gidx] = s_ve.n

            stats_pair(x1b, mmin)

            @block.vector
            def _(vector):
                vector.wait_ge(s_te.h, s_te.n)
                for k in range(16):
                    sl = slice(k * QC, (k + 1) * QC)
                    vector.tensor_tensor(lnx1b[:, sl], x1b[:, sl], pD[:, :],
                                         AluOpType.subtract)
                vector.drain()
                for k in range(16):
                    sl = slice(k * QC, (k + 1) * QC)
                    ins = vector.tensor_tensor(lnx1b[:, sl], lnx1b[:, sl], pH[:, :],
                                               AluOpType.mult)
                s_ve.inc(ins)

            stats_pair(lnx1b, mmin)

            @block.vector
            def _(vector):
                vector.wait_ge(s_te.h, s_te.n)
                for k in range(16):
                    sl = slice(k * QC, (k + 1) * QC)
                    vector.tensor_tensor(mmin[:, sl], lnx1b[:, sl], pD[:, :],
                                         AluOpType.subtract)
                vector.drain()
                for k in range(16):
                    sl = slice(k * QC, (k + 1) * QC)
                    vector.tensor_tensor(mmin[:, sl], mmin[:, sl], pH[:, :],
                                         AluOpType.mult)
                vector.drain()
                for k in range(16):
                    sl = slice(k * QC, (k + 1) * QC)
                    ins = vector.tensor_scalar(mmin[:, sl], mmin[:, sl],
                                               ln1w[:, k:k + 1], ln1bb[:, k:k + 1],
                                               AluOpType.mult, AluOpType.add)
                s_ve.inc(ins)

            if STAGE < 5: break
            mm_stream(qkvT_h, QKV, 24, mmin, 16, evac_qkv)

            if STAGE < 6: break
            for h in range(16):
                g = h // 4
                tq, tk, tv = 6 * g + h % 4, 6 * g + 4, 6 * g + 5

                @block.vector
                def _(vector, h=h, tq=tq, tk=tk):
                    vector.wait_ge(s_sc.h, s_sc.n)
                    if h >= 1:
                        vector.wait_ge(s_te.h, s_te.n)
                    vector.tensor_tensor(prodb[0][:, :], qb[:, tq * QC + HC:(tq + 1) * QC],
                                         qb[:, tk * QC:tk * QC + HC], AluOpType.mult)
                    s_ve.inc(vector.tensor_tensor(
                        prodb[1][:, :], qb[:, tq * QC + HC:(tq + 1) * QC],
                        qb[:, tk * QC + HC:(tk + 1) * QC], AluOpType.mult))

                @block.tensor
                def _(tensor, h=h):
                    tensor.wait_ge(s_ve.h, s_ve.n)
                    tensor.matmul(pE[0:16, 0:HC], hsel_sb[:, 16 * h:16 * (h + 1)],
                                  prodb[0][:, :], start=(h == 0), stop=(h == 15))
                    ins = tensor.matmul(pF[0:16, 0:HC], hsel_sb[:, 16 * h:16 * (h + 1)],
                                        prodb[1][:, :], start=(h == 0), stop=(h == 15))
                    s_te.inc(ins)

            @block.vector
            def _(vector):
                vector.wait_ge(s_te.h, s_te.n)
                vector.tensor_copy(att[:, 0:HC], pE[0:16, 0:HC])
                vector.drain()
                vector.tensor_tensor(att[:, 5 * HC:6 * HC], att[:, 0:HC],
                                     pF[0:16, 0:HC], AluOpType.max)
                vector.drain()
                vector.tensor_tensor(att[:, HC:2 * HC], att[:, 0:HC],
                                     att[:, 5 * HC:6 * HC], AluOpType.subtract)
                s_ve.inc(vector.tensor_tensor(att[:, 2 * HC:3 * HC], pF[0:16, 0:HC],
                                              att[:, 5 * HC:6 * HC],
                                              AluOpType.subtract))

            @block.scalar
            def _(scalar):
                scalar.wait_ge(s_ve.h, s_ve.n)
                scalar.activation(att[:, 3 * HC:4 * HC], att[:, HC:2 * HC], AF.Exp)
                s_sc.inc(scalar.activation(att[:, 4 * HC:5 * HC], att[:, 2 * HC:3 * HC],
                                           AF.Exp))

            @block.vector
            def _(vector):
                vector.wait_ge(s_sc.h, s_sc.n)
                vector.tensor_tensor(att[:, 5 * HC:6 * HC], att[:, 3 * HC:4 * HC],
                                     att[:, 4 * HC:5 * HC], AluOpType.add)
                vector.drain()
                vector.reciprocal(att[:, 0:HC], att[:, 5 * HC:6 * HC])
                vector.drain()
                vector.tensor_tensor(att[:, HC:2 * HC], att[:, 3 * HC:4 * HC],
                                     att[:, 0:HC], AluOpType.mult)
                s_ve.inc(vector.tensor_tensor(att[:, 2 * HC:3 * HC],
                                              att[:, 4 * HC:5 * HC],
                                              att[:, 0:HC], AluOpType.mult))

            @block.sync
            def _(sync):
                sync.wait_ge(s_ve.h, s_ve.n)
                s_aw.inc(sync.dma_start(
                    out=bass.AP(attw_d, 0, [[HC, 16], [1, HC]]),
                    in_=att[:, HC:2 * HC]), 16)
                s_aw.inc(sync.dma_start(
                    out=bass.AP(attw_d, 16 * HC, [[HC, 16], [1, HC]]),
                    in_=att[:, 2 * HC:3 * HC]), 16)

            @block.gpsimd
            def _(g, q=q):
                g.wait_ge(s_aw.h, s_aw.n)
                if q > 0:
                    g.wait_ge(s_ve.h, s_ve.n)
                s_awg.inc(g.dma_start(
                    out=wbb[:, :],
                    in_=bass.AP(attw_d, 0, [[0, 128], [HC, 16], [1, HC]])), 16)

            @block.vector
            def _(vector, q=q):
                vector.wait_ge(s_awg.h, s_awg.n)
                for h in range(16):
                    g2 = h // 4
                    tv = 6 * g2 + 5
                    ins = vector.tensor_tensor(
                        yg[:, h * QC:h * QC + HC], wbb[:, h * HC:(h + 1) * HC],
                        qb[:, tv * QC:tv * QC + HC], AluOpType.mult)
                s_ve.inc(ins)

            @block.gpsimd
            def _(g):
                g.wait_ge(s_ve.h, s_ve.n)
                s_awg.inc(g.dma_start(
                    out=wbb[:, :],
                    in_=bass.AP(attw_d, 16 * HC, [[0, 128], [HC, 16], [1, HC]])), 16)

            @block.vector
            def _(vector):
                vector.wait_ge(s_awg.h, s_awg.n)
                for h in range(16):
                    g2 = h // 4
                    tv = 6 * g2 + 5
                    vector.tensor_tensor(gelb[0][:, 0:HC], wbb[:, h * HC:(h + 1) * HC],
                                         qb[:, tv * QC + HC:(tv + 1) * QC],
                                         AluOpType.mult)
                    vector.drain()
                    vector.tensor_tensor(yg[:, h * QC + HC:(h + 1) * QC],
                                         yg[:, h * QC:h * QC + HC],
                                         gelb[0][:, 0:HC], AluOpType.add)
                    vector.drain()
                    ins = vector.tensor_copy(yg[:, h * QC:h * QC + HC],
                                             qb[:, tv * QC:tv * QC + HC])
                s_ve.inc(ins)

            if STAGE < 7: break
            mm_stream(projT_h, D, 16, yg, 16, evac_proj)

            @block.vector
            def _(vector):
                vector.wait_ge(s_ve.h, s_ve.n)
                for k in range(16):
                    ins = vector.tensor_copy(mmin[:, k * QC:(k + 1) * QC],
                                             tmph[:, k * QC:(k + 1) * QC])
                s_ve.inc(ins)
            stats_pair(mmin, yg)

            @block.vector
            def _(vector):
                vector.wait_ge(s_te.h, s_te.n)
                for k in range(16):
                    sl = slice(k * QC, (k + 1) * QC)
                    vector.tensor_tensor(tmph[:, sl], tmph[:, sl], x1b[:, sl],
                                         AluOpType.add)
                vector.drain()
                for k in range(16):
                    sl = slice(k * QC, (k + 1) * QC)
                    vector.tensor_tensor(mmin[:, sl], mmin[:, sl], pD[:, :],
                                         AluOpType.subtract)
                vector.drain()
                for k in range(16):
                    sl = slice(k * QC, (k + 1) * QC)
                    vector.tensor_tensor(mmin[:, sl], mmin[:, sl], pH[:, :],
                                         AluOpType.mult)
                vector.drain()
                for k in range(16):
                    sl = slice(k * QC, (k + 1) * QC)
                    ins = vector.tensor_scalar(mmin[:, sl], mmin[:, sl],
                                               ln2w[:, k:k + 1], ln2bb[:, k:k + 1],
                                               AluOpType.mult, AluOpType.add)
                s_ve.inc(ins)

            if STAGE < 9: break
            for tg in range(4):
                @block.tensor
                def _(tensor, tg=tg):
                    tensor.wait_ge(s_ve.h, s_ve.n)
                    for k in range(16):
                        ins = tensor.matmul(pG[0:128, 8 * tg:8 * (tg + 1)],
                                            mmin[:, k * QC + 128 * tg:k * QC + 128 * (tg + 1)],
                                            rw_sb[:, 8 * k:8 * (k + 1)],
                                            start=(k == 0), stop=(k == 15))
                    s_te.inc(ins)

                @block.vector
                def _(vector, tg=tg):
                    vector.wait_ge(s_te.h, s_te.n)
                    pgb = pG[0:128, 8 * tg:8 * (tg + 1)]
                    vector.tensor_copy(gtok[:, 0:E], pgb)
                    vector.drain()
                    vector.tensor_reduce(gcol[:, 0:1], gtok[:, 0:E], AX.X,
                                         AluOpType.max)
                    vector.drain()
                    vector.tensor_scalar(gtok[:, E:2 * E], gtok[:, 0:E],
                                         gcol[:, 0:1], None, AluOpType.is_ge)
                    vector.drain()
                    vector.scalar_tensor_tensor(gtok[:, 2 * E:3 * E],
                                                gtok[:, E:2 * E], -1e30,
                                                gtok[:, 0:E],
                                                AluOpType.mult, AluOpType.add)
                    vector.drain()
                    vector.tensor_reduce(gcol[:, 1:2], gtok[:, 2 * E:3 * E], AX.X,
                                         AluOpType.max)
                    s_ve.inc(vector.tensor_scalar(gcol[:, 4:5], gcol[:, 0:1], -1.0,
                                                  None, AluOpType.mult))

                @block.scalar
                def _(scalar, tg=tg):
                    scalar.wait_ge(s_ve.h, s_ve.n)
                    s_sc.inc(scalar.activation(gtok[:, 2 * E:3 * E], gtok[:, 0:E],
                                               AF.Exp, bias=gcol[:, 4:5]))

                @block.vector
                def _(vector, tg=tg):
                    vector.wait_ge(s_sc.h, s_sc.n)
                    vector.tensor_scalar(gtok[:, E:2 * E], gtok[:, 0:E],
                                         gcol[:, 1:2], None, AluOpType.is_ge)
                    vector.drain()
                    vector.tensor_tensor(gtok[:, 3 * E:4 * E], gtok[:, 2 * E:3 * E],
                                         gtok[:, E:2 * E], AluOpType.mult)
                    vector.drain()
                    vector.tensor_reduce(gcol[:, 2:3], gtok[:, 3 * E:4 * E], AX.X,
                                         AluOpType.add)
                    vector.drain()
                    vector.reciprocal(gcol[:, 3:4], gcol[:, 2:3])
                    vector.drain()
                    s_ve.inc(vector.tensor_scalar(gtok[:, 4 * E:5 * E],
                                                  gtok[:, 3 * E:4 * E],
                                                  gcol[:, 3:4], None, AluOpType.mult))

                @block.tensor
                def _(tensor, tg=tg):
                    tensor.wait_ge(s_ve.h, s_ve.n)
                    s_te.inc(tensor.transpose(pH[0:8, 128 * tg:128 * (tg + 1)],
                                              gtok[:, 4 * E:5 * E], id_sb[:, :]))

                @block.vector
                def _(vector, tg=tg):
                    vector.wait_ge(s_te.h, s_te.n)
                    s_ve.inc(vector.tensor_copy(gater[:, 128 * tg:128 * (tg + 1)],
                                                pH[0:8, 128 * tg:128 * (tg + 1)]))

            @block.sync
            def _(sync):
                sync.wait_ge(s_ve.h, s_ve.n)
                s_aw.inc(sync.dma_start(out=gat_d.ap(), in_=gater[:, :]), 16)

            if STAGE < 10: break
            te_w1p = {}
            scg = {}
            ttd = {}
            ve_accd = {}
            xf_ready = s_ve.n
            for e in range(NE):
                @block.gpsimd
                def _(g, e=e, q=q):
                    g.wait_ge(s_aw.h, s_aw.n)
                    if e >= 2:
                        g.wait_ge(s_ve.h, ttd[(e - 2, 15)])
                    s_gb[e % 2].inc(g.dma_start(
                        out=gbsb[e % 2][:, :],
                        in_=bass.AP(gat_d, e * QC, [[0, 128], [1, QC]])), 16)

                for pp in range(16):
                    @block.gpsimd
                    def _(g, e=e, pp=pp, q=q):
                        if pp >= 2:
                            g.wait_ge(s_te.h, te_w1p[(e, pp - 2)])
                        elif e >= 1:
                            g.wait_ge(s_te.h, te_w2d[(e - 1, 14 + pp)])
                        else:
                            g.wait_ge(s_ve.h, xf_ready)
                            if (7, 14 + pp) in te_w2d:
                                g.wait_ge(s_te.h, te_w2d[(7, 14 + pp)])
                        s_wp[pp % 2].inc(g.dma_start(
                            out=qb[:, (pp % 2) * 8 * QC:(pp % 2) * 8 * QC + 8 * QC],
                            in_=bass.AP(w1_h, e * D * FFN + 256 * pp,
                                        [[FFN, 128], [128 * FFN, 16], [1, 256]])), 16)

                    @block.tensor
                    def _(tensor, e=e, pp=pp, xf_ready=xf_ready):
                        tensor.wait_ge(s_wp[pp % 2].h, s_wp[pp % 2].n)
                        if pp >= 2:
                            tensor.wait_ge(s_sc.h, scg[(e, pp - 2)])
                        elif e >= 1:
                            tensor.wait_ge(s_ve.h, ve_accd[(e - 1, 14 + pp)])
                        else:
                            tensor.wait_ge(s_ve.h, xf_ready)
                        bks = [(pB, pC), (pD, pH)][pp % 2]
                        base = (pp % 2) * 8 * QC
                        for c2 in range(2):
                            for k in range(16):
                                ins = tensor.matmul(
                                    bks[c2][:, :],
                                    qb[:, base + k * 2 * 128 + c2 * 128:
                                        base + k * 2 * 128 + c2 * 128 + 128],
                                    mmin[:, k * QC:(k + 1) * QC],
                                    start=(k == 0), stop=(k == 15))
                        s_te.inc(ins)
                    te_w1p[(e, pp)] = s_te.n

                    @block.scalar
                    def _(scalar, e=e, pp=pp):
                        scalar.wait_ge(s_te.h, s_te.n)
                        if pp >= 2:
                            scalar.wait_ge(s_ve.h, ttd[(e, pp - 2)])
                        bks = [(pB, pC), (pD, pH)][pp % 2]
                        gb2 = 2 * (pp % 2)
                        scalar.activation(gelb[gb2][:, :], bks[0][:, :], AF.Gelu)
                        ins = scalar.activation(gelb[gb2 + 1][:, :], bks[1][:, :],
                                                AF.Gelu)
                        s_sc.inc(ins)
                    scg[(e, pp)] = s_sc.n

                    @block.vector
                    def _(vector, e=e, pp=pp):
                        vector.wait_ge(s_sc.h, s_sc.n)
                        vector.wait_ge(s_gb[e % 2].h, s_gb[e % 2].n)
                        gb2 = 2 * (pp % 2)
                        vector.tensor_tensor(yg[:, 2 * pp * QC:(2 * pp + 1) * QC],
                                             gelb[gb2][:, :], gbsb[e % 2][:, :],
                                             AluOpType.mult)
                        ins = vector.tensor_tensor(
                            yg[:, (2 * pp + 1) * QC:(2 * pp + 2) * QC],
                            gelb[gb2 + 1][:, :], gbsb[e % 2][:, :], AluOpType.mult)
                        s_ve.inc(ins)
                    ttd[(e, pp)] = s_ve.n

                for o in range(16):
                    @block.gpsimd
                    def _(g, e=e, o=o):
                        if o >= 2:
                            g.wait_ge(s_te.h, te_w2d[(e, o - 2)])
                        else:
                            g.wait_ge(s_te.h, te_w1p[(e, 14 + o)])
                        s_w2[o % 2].inc(g.dma_start(
                            out=qb[:, (o % 2) * 8 * QC:(o % 2) * 8 * QC + 8 * QC],
                            in_=bass.AP(w2_h, e * FFN * D + 128 * o,
                                        [[D, 128], [128 * D, 32], [1, 128]])), 16)

                    @block.tensor
                    def _(tensor, e=e, o=o):
                        tensor.wait_ge(s_w2[o % 2].h, s_w2[o % 2].n)
                        tensor.wait_ge(s_ve.h, ttd[(e, 15)])
                        if o >= 2:
                            tensor.wait_ge(s_ve.h, ve_accd[(e, o - 2)])
                        base = (o % 2) * 8 * QC
                        pdst = pB if o % 2 == 0 else pC
                        for k in range(32):
                            ins = tensor.matmul(pdst[:, :],
                                                qb[:, base + 128 * k:base + 128 * (k + 1)],
                                                yg[:, k * QC:(k + 1) * QC],
                                                start=(k == 0), stop=(k == 31))
                        s_te.inc(ins)
                    te_w2d[(e, o)] = s_te.n

                    @block.vector
                    def _(vector, e=e, o=o):
                        vector.wait_ge(s_te.h, s_te.n)
                        s_ve.inc(vector.tensor_tensor(
                            tmph[:, o * QC:(o + 1) * QC], tmph[:, o * QC:(o + 1) * QC],
                            (pB if o % 2 == 0 else pC)[:, :], AluOpType.add))
                    ve_accd[(e, o)] = s_ve.n

            if STAGE < 11: break
            osub = {}
            for tg2 in range(4):
                for sub in range(4):
                    @block.tensor
                    def _(tensor, tg2=tg2, q=q, sub=sub):
                        if sub == 0:
                            tensor.wait_ge(s_ve.h, s_ve.n)
                            if tg2 >= 2:
                                tensor.wait_ge(s_ow.h, 16 * (4 * q + tg2 - 1))
                        if True:
                            tensor.wait_ge(s_sc.h, osub.get((tg2, sub - 1), s_sc.n if sub == 0 else 0))
                        for kk in range(4):
                            k = 4 * sub + kk
                            ins = tensor.transpose(
                                pA[:, 128 * kk:128 * kk + 128],
                                tmph[:, k * QC + 128 * tg2:k * QC + 128 * (tg2 + 1)],
                                id_sb[:, :])
                        s_te.inc(ins)

                    @block.scalar
                    def _(scalar, tg2=tg2, sub=sub):
                        scalar.wait_ge(s_te.h, s_te.n)
                        for kk in range(4):
                            k = 4 * sub + kk
                            ins = scalar.copy(xrow[tg2 % 2][:, 128 * k:128 * (k + 1)],
                                              pA[:, 128 * kk:128 * kk + 128])
                        s_sc.inc(ins)
                    osub[(tg2, sub)] = s_sc.n

                @block.sync
                def _(sync, tg2=tg2, q=q):
                    sync.wait_ge(s_sc.h, s_sc.n)
                    s_i = tg2 // 2
                    r0 = HC * q + 128 * (tg2 % 2)
                    s_ow.inc(sync.dma_start(
                        out=bass.AP(out_h, (s_i * RPC + r0) * D, [[D, 128], [1, D]]),
                        in_=xrow[tg2 % 2][:, :]), 16)

        @block.sync
        def _(sync):
            sync.wait_ge(s_ow.h, s_ow.n)

    return None



def pp16(v):
    return np.ascontiguousarray(np.asarray(v, np.float32).reshape(16, 128).T)


def hsel_const():
    h = np.zeros((128, 16, 16), np.float32)
    for hh in range(16):
        h[:, hh, hh] = SC128
    return np.ascontiguousarray(h.reshape(128, 256))


def make_in_maps_b(d, y_full):
    qkvT = np.ascontiguousarray(np.asarray(d["L0_ffn_qkv_weight"], np.float32).T)
    projT = np.ascontiguousarray(np.asarray(d["L0_ffn_proj_weight"], np.float32).T)
    ims = []
    x = np.asarray(d["L0_attn_x"], np.float32)
    for r in range(8):
        rs = slice(RPC * r, RPC * (r + 1))
        ims.append({
            "x_rows": np.ascontiguousarray(x[:, rs, :]),
            "qkvT": qkvT,
            "projT": projT,
            "router_w": np.ascontiguousarray(d["L0_ffn_router_weight"], dtype=np.float32),
            "moe_w1": np.ascontiguousarray(d["L0_ffn_moe_w1"], dtype=np.float32),
            "moe_w2": np.ascontiguousarray(d["L0_ffn_moe_w2"], dtype=np.float32),
            "ln1w_pp": pp16(d["L0_ffn_ln1_weight"]),
            "ln1b_pp": pp16(d["L0_ffn_ln1_bias"]),
            "ln2w_pp": pp16(d["L0_ffn_ln2_weight"]),
            "ln2b_pp": pp16(d["L0_ffn_ln2_bias"]),
            "ident2": np.eye(128, dtype=np.float32),
            "hsel": hsel_const(),
        })
    return ims




def build_fused():
    nc = bass.Bass("TRN2", target_bir_lowering=False, detect_race_conditions=False)
    with ExitStack() as semctx:
        s_cc = semctx.enter_context(nc.semaphore("s_cc"))
        y_h, s_yw = emit_parta(nc, semctx)
        yag_h = nc.dram_tensor("yag", [D, B, T], F32)
        with nc.Block() as blk:
            @blk.gpsimd
            def _(g):
                g.wait_ge(s_yw, 16 * 4 * NCH)
                g.collective_compute(
                    "AllGather", AluOpType.bypass,
                    replica_groups=[list(range(8))],
                    ins=[y_h.ap().opt()], outs=[yag_h.ap().opt()],
                ).then_inc(s_cc, 1)
        emit_partb(nc, semctx, yag_h, s_cc)
    return nc


def kernel(**inputs):
    d = inputs
    ims_a = make_in_maps_a(d)
    ims_b = make_in_maps_b(d, None)
    ims = [{**a, **b} for a, b in zip(ims_a, ims_b)]
    nc = build_fused()
    res = run_bass_kernel_spmd(nc, ims, core_ids=list(range(8)))
    out = np.concatenate([np.asarray(res.results[r]["out_r"]).reshape(2, RPC, D)
                          for r in range(8)], axis=1)
    return np.ascontiguousarray(out, dtype=np.float32)
